# version 26
# baseline (speedup 1.0000x reference)
import contextlib
import math
import numpy as np

import concourse.bass as bass
import concourse.mybir as mybir
from concourse.bass_utils import run_bass_kernel_spmd

F32 = mybir.dt.float32
BF16 = mybir.dt.bfloat16
AF = mybir.ActivationFunctionType
ALU = mybir.AluOpType
AX = mybir.AxisListType

S = 2048
D = 2048
NT = 16
ND = 16
NH = 16
EI = 1408
NIB = 11
NPE = 10
QSCALE = 1.0 / math.sqrt(128.0)
EXPBIAS = -12.0
EPS = 1e-5

ENGINES = ("tensor", "vector", "scalar", "gpsimd", "sync")

RESV = 16512
SMALL = RESV
WB = SMALL + 2048
XLNT_OFF = WB + 45056
R1_OFF = XLNT_OFF + 65536
SPARE = R1_OFF + 65536

PHASES = ("p0", "p1", "p2", "p3", "p3b", "p5a", "p5b")

SEM_NAMES = (
    ["s_t", "s_v", "s_s", "s_g"]
    + ["w_c", "w_cs", "w_k", "w_v", "w_qc0", "w_qc1", "w_wo0", "w_wo1",
       "w_e0", "w_e1", "w_gr0", "w_gr1", "w_w20", "w_w21"]
    + ["l_xt0", "l_xt1", "l_xt2", "l_xs0", "l_xs1", "l_xs2",
       "l_xq0", "l_xq1", "l_xq2", "w_wo2", "l_h0", "l_h1",
       "l_asl0", "l_asl1", "l_asl2", "l_asl3", "l_yin"]
    + ["st_kq0", "st_kq1", "st_v0", "st_v1", "st_x20", "st_x21", "st_x22",
       "st_gsp", "st_af0", "st_af1", "st_out", "st_tap"]
)
DMA_SEMS = [s for s in SEM_NAMES
            if (s[0] in "wl" and s != "s_g") or s.startswith("st")]


class Prog:
    def __init__(self, nc):
        self.nc = nc
        self.ops = {e: [] for e in ENGINES}
        self.semh = {}
        self.cnt = {}
        self.phase_waits = []

    def add_sem(self, name, handle):
        self.semh[name] = handle
        self.cnt[name] = 0

    def phase(self, waits):
        self.phase_waits = [w for w in waits if w is not None]

    def emit(self, eng, fn, waits=(), inc=None, dma=False):
        amt = 16 if dma else 1
        tok = None
        incs = []
        if inc is not None:
            self.cnt[inc] += amt
            tok = (inc, self.cnt[inc])
            incs.append((inc, amt))
        w = self.phase_waits + [x for x in waits if x is not None]
        self.ops[eng].append([fn, w, incs])
        return tok

    def mark(self, eng, sem):
        assert self.ops[eng], f"no ops on {eng} to mark"
        rec = self.ops[eng][-1]
        for (s, _a) in rec[2]:
            if s == sem:
                return (sem, self.cnt[sem])
        self.cnt[sem] += 1
        rec[2].append((sem, 1))
        return (sem, self.cnt[sem])

    def tokt(self, sem):
        return (sem, self.cnt[sem])

    def all_dma_toks(self):
        return [(s, self.cnt[s]) for s in DMA_SEMS if self.cnt[s] > 0]

    def replay(self, block):
        for eng_name in ENGINES:
            section = getattr(block, eng_name)
            oplist = self.ops[eng_name]
            semh = self.semh

            def make(oplist=oplist):
                lastwait = {}

                def _(eng):
                    for fn, waits, incs in oplist:
                        for (s, v) in waits:
                            if v > 0 and lastwait.get(s, -1) < v:
                                eng.wait_ge(semh[s], v)
                                lastwait[s] = v
                        inst = fn(eng)
                        if inst is None:
                            assert not incs
                            continue
                        for (s, amt) in incs:
                            r = inst.then_inc(semh[s], amt)
                            if r is not None:
                                inst = r
                return _

            section(make())


def build(taps=(), upto="p5b"):
    nupto = PHASES.index(upto)
    nc = bass.Bass("TRN2", target_bir_lowering=False)
    assert nc.sbuf_base <= SMALL, f"bass internal allocs reached {nc.sbuf_base}"

    xin = nc.dram_tensor("xin", [S, D], F32, kind="ExternalInput")
    wq = nc.dram_tensor("wq", [D, D], F32, kind="ExternalInput")
    wk = nc.dram_tensor("wk", [D, D], F32, kind="ExternalInput")
    wv = nc.dram_tensor("wv", [D, D], F32, kind="ExternalInput")
    wo = nc.dram_tensor("wo", [D, D], F32, kind="ExternalInput")
    gwT = nc.dram_tensor("gwT", [D, 8], F32, kind="ExternalInput")
    w1 = nc.dram_tensor("w1", [8, D, EI], F32, kind="ExternalInput")
    w3 = nc.dram_tensor("w3", [8, D, EI], F32, kind="ExternalInput")
    w2 = nc.dram_tensor("w2", [8, EI, D], F32, kind="ExternalInput")
    sw1 = nc.dram_tensor("sw1", [D, 2 * EI], F32, kind="ExternalInput")
    sw3 = nc.dram_tensor("sw3", [D, 2 * EI], F32, kind="ExternalInput")
    sw2 = nc.dram_tensor("sw2", [2 * EI, D], F32, kind="ExternalInput")
    cident = nc.dram_tensor("cident", [128, 128], F32, kind="ExternalInput")
    ctriu = nc.dram_tensor("ctriu", [128, 128], F32, kind="ExternalInput")
    cperm = nc.dram_tensor("cperm", [128, 128], F32, kind="ExternalInput")
    ccos = nc.dram_tensor("ccos", [128, S], F32, kind="ExternalInput")
    cssin = nc.dram_tensor("cssin", [128, S], F32, kind="ExternalInput")
    out = nc.dram_tensor("out", [S, D], F32, kind="ExternalOutput")

    QT_sp = nc.dram_tensor("QT_sp", [D, S], BF16, kind="Internal")
    KT_sp = nc.dram_tensor("KT_sp", [D, S], BF16, kind="Internal")
    V_sp = nc.dram_tensor("V_sp", [S, D], BF16, kind="Internal")
    X2_sp = nc.dram_tensor("X2_sp", [S, D], F32, kind="Internal")
    AT_sp = nc.dram_tensor("AT_sp", [NPE, NIB, 128, S], BF16, kind="Internal")
    GSP = nc.dram_tensor("GSP", [8, S], F32, kind="Internal")

    tap_outs = {}
    tap_specs = {"XLNT_t": ([128, 16 * 2048], BF16), "QT_t": ([D, S], BF16),
                 "KT_t": ([D, S], BF16), "V_t": ([S, D], BF16),
                 "ATT_t": ([128, 16 * 2048], BF16), "X2_t": ([S, D], F32),
                 "GSP_t": ([8, S], F32), "STATS_t": ([128, 48], F32),
                 "XLNT2_t": ([128, 16 * 2048], BF16),
                 "AT_t": ([NPE, NIB, 128, S], BF16)}
    for t in taps:
        shp, dt = tap_specs[t]
        tap_outs[t] = nc.dram_tensor(t, shp, dt, kind="ExternalOutput")

    A = nc.alloc_sbuf_tensor_at
    IDF = A("IDF", [128, 128], F32, offset=SMALL + 0)
    TRIUB = A("TRIUB", [128, 128], BF16, offset=SMALL + 512)
    PERMB = A("PERMB", [128, 128], BF16, offset=SMALL + 768)
    ONESC = A("ONESC", [128, 1], BF16, offset=SMALL + 1024)
    ONESR = A("ONESR", [1, 128], BF16, offset=SMALL + 1056)
    GWTB = A("GWTB", [128, 128], BF16, offset=SMALL + 1312)
    STATS = A("STATS", [128, 64], F32, offset=SMALL + 1568)
    EXB = A("EXB", [128, 1], F32, offset=SMALL + 1824)

    xt = [A("xt0", [128, D], F32, offset=WB + 0),
          A("xt1", [128, D], F32, offset=WB + 8192),
          A("xt2", [128, D], F32, offset=SPARE + 8192)]
    sqb = [A("sqb0", [128, D], BF16, offset=SPARE + 0),
           A("sqb1", [128, D], BF16, offset=SPARE + 4096)]
    gwt = [A("gwt0", [128, 8], F32, offset=SPARE + 33024),
           A("gwt1", [128, 8], F32, offset=SPARE + 33056)]
    gt8 = A("gt8", [8, 128], F32, offset=SPARE + 33088)
    IDB = A("IDB", [128, 128], BF16, offset=SPARE + 32768)
    SPWOQ = A("SPWOQ", [128, 16 * 512], BF16, offset=SPARE + 0)

    kr = [A("kr0", [128, S], BF16, offset=WB + 0),
          A("kr1", [128, S], BF16, offset=WB + 4096)]
    t1b = A("t1b", [128, S], BF16, offset=WB + 8192)
    t2b = A("t2b", [128, S], BF16, offset=WB + 12288)
    krp = [A("krp0", [128, S], BF16, offset=WB + 16384),
           A("krp1", [128, S], BF16, offset=WB + 20480)]
    qc = [A("qc0", [128, S], BF16, offset=WB + 24576),
          A("qc1", [128, S], BF16, offset=WB + 28672)]
    COSB = A("COSB", [128, S], BF16, offset=WB + 32768)
    SSINB = A("SSINB", [128, S], BF16, offset=WB + 36864)

    qtn = [A("qtn0", [128, S], BF16, offset=WB + 0),
           A("qtn1", [128, S], BF16, offset=WB + 4096)]
    ktn = [A("ktn0", [128, S], BF16, offset=WB + 8192),
           A("ktn1", [128, S], BF16, offset=WB + 12288)]
    vn = [A("vn0", [128, S], BF16, offset=WB + 16384),
          A("vn1", [128, S], BF16, offset=WB + 20480)]
    expt = [A("exp0", [128, 512], BF16, offset=WB + 24576),
            A("exp1", [128, 512], BF16, offset=WB + 25600),
            A("exp2", [128, 512], BF16, offset=WB + 41984),
            A("exp3", [128, 512], BF16, offset=WB + 43008)]
    bcs = A("bcs", [128, 1024], BF16, offset=WB + 26624)
    zsb = A("zsb", [1, 1024], F32, offset=WB + 28672)
    recf = A("recf", [1, 1024], F32, offset=WB + 32768)
    recb = A("recb", [1, 1024], BF16, offset=WB + 36864)

    woq = [A("woq0", [128, 16 * 512], BF16, offset=WB + 0),
           A("woq1", [128, 16 * 512], BF16, offset=WB + 16384)]
    xq = [A("xq0", [128, 512], F32, offset=WB + 32768),
          A("xq1", [128, 512], F32, offset=WB + 34816),
          A("xq2", [128, 512], F32, offset=WB + 36864)]
    x2q = [A("x2q0", [128, 512], F32, offset=WB + 38912),
           A("x2q1", [128, 512], F32, offset=WB + 40960),
           A("x2q2", [128, 512], F32, offset=WB + 43008)]

    w1c = [A("w1c0", [128, S], BF16, offset=SPARE + 16384),
           A("w1c1", [128, S], BF16, offset=SPARE + 24576)]
    w3c = [A("w3c0", [128, S], BF16, offset=SPARE + 20480),
           A("w3c1", [128, S], BF16, offset=SPARE + 28672)]
    silu = A("silu", [128, S], BF16, offset=WB + 16384)
    atmp = A("atmp", [128, S], BF16, offset=WB + 20480)
    afin = [A("afin0", [128, S], BF16, offset=WB + 24576),
            A("afin1", [128, S], BF16, offset=WB + 28672)]
    beb = [A("beb0", [128, S], BF16, offset=WB + 32768),
           A("beb1", [128, S], BF16, offset=WB + 0)]
    grb = [A("gr0", [1, S], BF16, offset=WB + 36864),
           A("gr1", [1, S], BF16, offset=WB + 40960)]

    aslg = [A("aslg0", [128, NIB * 512], BF16, offset=WB + 4096),
            A("aslg1", [128, NIB * 512], BF16, offset=WB + 15360)]
    x2b = [A("x2b0", [128, 1024], F32, offset=WB + 0),
           A("x2b1", [128, 1024], F32, offset=WB + 26624)]

    XLNT = A("XLNT", [128, 16 * 2048], BF16, offset=XLNT_OFF)
    Yacc = [A(f"Yacc{k}", [128, 1024], F32, offset=XLNT_OFF + 4096 * k)
            for k in range(NT)]
    WKV = A("WKV", [128, 16 * 2048], BF16, offset=R1_OFF)
    ATTNT = A("ATTNT", [128, 16 * 2048], BF16, offset=R1_OFF)
    w2s = [A("w2s0", [128, NIB * 1024], BF16, offset=R1_OFF + 0),
           A("w2s1", [128, NIB * 1024], BF16, offset=R1_OFF + 22528)]

    pA = nc.alloc_psum_tensor("pA", [128, 2048], F32)
    pB = nc.alloc_psum_tensor("pB", [128, 2048], F32)

    with contextlib.ExitStack() as stack:
        p = Prog(nc)
        for name in SEM_NAMES:
            p.add_sem(name, stack.enter_context(nc.semaphore(name)))
        block = stack.enter_context(nc.Block())
        emit = p.emit
        mark = p.mark

        def barrier(extra_st=False):
            w = [mark("tensor", "s_t"), mark("vector", "s_v"), mark("scalar", "s_s")]
            if extra_st:
                w += p.all_dma_toks()
            return w

        p.phase([])
        t_consts = None
        for (dstt, srct) in ((IDF, cident), (TRIUB, ctriu), (PERMB, cperm)):
            t_consts = emit("gpsimd",
                            lambda e, d=dstt, s=srct: e.dma_start(d[:, :], s[:, :]),
                            inc="w_c", dma=True)
        gsrc = gwT.rearrange("(a p) e -> p a e", p=128)
        t_consts = emit("gpsimd",
                        lambda e: e.dma_start(
                            GWTB.ap().rearrange("p (a c) -> p a c", a=16)[:, :, 0:8],
                            gsrc),
                        inc="w_c", dma=True)
        t_wk = emit("gpsimd",
                    lambda e: e.dma_start(
                        WKV.ap().rearrange("p (a n) -> p a n", a=16),
                        wk.rearrange("(a p) n -> p a n", p=128)),
                    inc="w_k", dma=True)
        emit("vector", lambda e: e.memset(ONESC[:, :], 1.0))
        emit("vector", lambda e: e.memset(ONESR[:, :], 1.0))
        emit("vector", lambda e: e.memset(EXB[:, :], EXPBIAS))
        t_ones = mark("vector", "s_v")
        t_idb = emit("scalar", lambda e: e.activation(IDB[:, :], IDF[:, :], AF.Copy),
                     waits=[p.tokt("w_c")], inc="s_s")

        def vchain(fn, extra=()):
            return emit("vector", fn, waits=[p.tokt("s_v")] + list(extra), inc="s_v")

        LNCOLS = [[0, 1, 2, 3, 4, 6, 7, 8], [9, 10, 11, 12, 13, 14, 15, 24]]

        def c1(col):
            return STATS[:, col:col + 1]

        def ln_stats(xtile, dstb, src_toks, sq_waits, C):
            tsq = emit("scalar", lambda e: e.activation(dstb[:, :], xtile[:, :], AF.Square,
                                                        accum_out=c1(C[1])),
                       waits=list(src_toks) + list(sq_waits), inc="s_s")
            tsum = vchain(lambda e: e.reduce_sum(c1(C[0]), xtile[:, :], axis=AX.X),
                          extra=list(src_toks))
            return tsq, tsum

        def ln_rest(xtile, dstb, tsq, tsum, C):
            vchain(lambda e: e.tensor_scalar_mul(c1(C[2]), c1(C[0]), 1.0 / D),
                   extra=[tsum])
            vchain(lambda e: e.tensor_scalar(c1(C[3]), c1(C[1]), 1.0 / D, EPS,
                                             ALU.mult, ALU.add), extra=[tsq])
            vchain(lambda e: e.tensor_mul(c1(C[4]), c1(C[2]), c1(C[2])))
            tv = vchain(lambda e: e.tensor_sub(c1(C[5]), c1(C[3]), c1(C[4])))
            tln = emit("scalar", lambda e: e.activation(c1(C[6]), c1(C[5]), AF.Ln),
                       waits=[tv], inc="s_s")
            tr = emit("scalar", lambda e: e.activation(c1(C[7]), c1(C[6]), AF.Exp,
                                                       scale=-0.5), waits=[tln], inc="s_s")
            tnb = vchain(lambda e: e.scalar_tensor_tensor(c1(C[4]), c1(C[2]), -1.0,
                                                          c1(C[7]), ALU.mult, ALU.mult),
                         extra=[tr])
            emit("scalar", lambda e: e.activation(dstb[:, :], xtile[:, :], AF.Identity,
                                                  bias=c1(C[4]), scale=c1(C[7])),
                 waits=[tnb], inc="s_s")
            return p.tokt("s_s")

        def load_xt(i, par, srcten, extra=()):
            t1 = emit("sync", lambda e, i=i, par=par, srcten=srcten: e.dma_start(
                xt[par][:, :], srcten[i * 128:(i + 1) * 128, :]),
                waits=[xt_free[par]] + list(extra), inc=f"l_xt{par}", dma=True)
            return [t1]

        pAb = pA.ap().bitcast(BF16)
        pBb = pB.ap().bitcast(BF16)

        def emit_transposes(i, ev_prev, sink, sq):
            ev = [None, None, None, None]
            gco = (0, 2048, 2048, 3072)
            for g in range(4):
                psb = pAb if g < 2 else pBb
                coff = gco[g]
                for jj in range(4):
                    j = g * 4 + jj
                    emit("tensor", lambda e, psb=psb, coff=coff, jj=jj, j=j, sq=sq: e.matmul(
                        psb[:, coff + jj * 128: coff + (jj + 1) * 128],
                        sq[:, j * 128:(j + 1) * 128], IDB[:, :],
                        start=True, stop=True, is_transpose=True),
                        waits=[sink, ev_prev[g]], inc="s_t")
                tt = p.tokt("s_t")
                jbase = g * 4

                def evac(e, psb=psb, coff=coff, jbase=jbase, i=i):
                    dst = XLNT.ap().rearrange("p (a c) -> p a c", a=16)[
                        :, jbase:jbase + 4, i * 128:(i + 1) * 128]
                    return e.copy(dst, psb[:, coff:coff + 512].rearrange(
                        "p (a c) -> p a c", a=4))
                ev[g] = emit("scalar", evac, waits=[tt], inc="s_s")
            return ev, p.tokt("s_t")

        p.phase([p.tokt("w_c"), t_ones, t_idb])
        xt_free = [None, None, None]
        ev_prev = [None, None, None, None]
        sq_free = [None, None]
        pend_ln = None

        def ln_finish(pend):
            nonlocal ev_prev
            j, jpar, jbpar, jst = pend
            tx = ln_rest(xt[jpar], sqb[jbpar], *jst, LNCOLS[j % 2])
            xt_free[jpar] = tx
            ev_prev, sq_free[jbpar] = emit_transposes(j, ev_prev, tx, sqb[jbpar])

        lts_q = {0: load_xt(0, 0, xin), 1: load_xt(1, 1, xin)}
        for i in range(NT):
            par = i % 3
            bpar = i % 2
            if pend_ln is not None:
                ln_finish(pend_ln)
            if i + 2 < NT:
                lts_q[i + 2] = load_xt(i + 2, (i + 2) % 3, xin)
            st = ln_stats(xt[par], sqb[bpar], lts_q[i], [sq_free[bpar]], LNCOLS[i % 2])
            pend_ln = (i, par, bpar, st)
        ln_finish(pend_ln)
        pend_ln = None
        BAR0 = barrier()
        emit("gpsimd", lambda e: e.dma_start(COSB[:, :], ccos[:, :]),
             waits=BAR0, inc="w_cs", dma=True)
        cos_tok = emit("gpsimd", lambda e: e.dma_start(SSINB[:, :], cssin[:, :]),
                       inc="w_cs", dma=True)
        if "STATS_t" in tap_outs:
            emit("sync", lambda e: e.dma_start(tap_outs["STATS_t"][:, :], STATS[:, :]),
                 waits=BAR0, inc="st_tap", dma=True)
        if "XLNT_t" in tap_outs:
            emit("sync", lambda e: e.dma_start(tap_outs["XLNT_t"][:, :], XLNT[:, :]),
                 waits=BAR0, inc="st_tap", dma=True)
        if nupto < 1:
            finish(p, emit)
            p.replay(block)
            return nc

        wqr = wq.rearrange("(a p) n -> p a n", p=128)
        qctoks = [None] * NH
        TE = {}; TV2 = {}; TROPE = {}; TSP = {}; PSH = {}

        def proj_part(h, lhs_tile, lhs_kind, n, scale, wtok):
            par = h % 2
            ps = pA if par == 0 else pB
            for db in range(ND):
                lcol = db * 2048 + n * 128 if lhs_kind == "wkv" else db * 128
                for c in range(4):
                    emit("tensor", lambda e, db=db, c=c, lcol=lcol, ps=ps,
                         lhs_tile=lhs_tile: e.matmul(
                        ps[:, c * 512:(c + 1) * 512],
                        lhs_tile[:, lcol:lcol + 128],
                        XLNT[:, db * 2048 + c * 512: db * 2048 + (c + 1) * 512],
                        start=(db == 0), stop=(db == ND - 1)),
                        waits=[wtok], inc="s_t")
            tproj = p.tokt("s_t")
            TE[h] = emit("scalar", lambda e, par=par, scale=scale, ps=ps: e.activation(
                kr[par][:, :], ps[:, :], AF.Copy, scale=scale),
                waits=[tproj], inc="s_s")
            PSH[h] = ps

        def perm_part(h, n, spill):
            par = h % 2
            ps = PSH[h]
            for c in range(4):
                emit("tensor", lambda e, c=c, par=par, ps=ps: e.matmul(
                    ps[:, c * 512:(c + 1) * 512], PERMB[:, :],
                    kr[par][:, c * 512:(c + 1) * 512], start=True, stop=True),
                    waits=[TE[h]], inc="s_t")
            tperm = p.tokt("s_t")
            t1t = emit("vector", lambda e, par=par: e.tensor_mul(t1b[:, :], kr[par][:, :],
                                                                 COSB[:, :]),
                       waits=[TE[h], TROPE.get(h - 1)], inc="s_v")
            tv2 = emit("vector", lambda e, ps=ps: e.tensor_mul(t2b[:, :], ps[:, :],
                                                               SSINB[:, :]),
                       waits=[tperm], inc="s_v")
            trope = emit("vector", lambda e, par=par: e.tensor_add(krp[par][:, :], t1b[:, :],
                                                                   t2b[:, :]),
                         waits=[t1t, tv2, TSP.get(h - 2)], inc="s_v")
            tsp = emit("sync", lambda e, n=n, par=par, spill=spill: e.dma_start(
                spill[n * 128:(n + 1) * 128, :], krp[par][:, :]),
                waits=[trope], inc=f"st_kq{par}", dma=True)
            TV2[h] = tv2
            TROPE[h] = trope
            TSP[h] = tsp

        t_wv = None
        for h in range(2 * NH):
            n = h % NH
            p.phase(BAR0 + [cos_tok] + [t for t in [TV2.get(h - 2)] if t])
            if h == 0:
                for m in range(2):
                    qctoks[m] = emit("gpsimd", lambda e, m=m: e.dma_start(
                        qc[m % 2].ap().rearrange("p (a c) -> p a c", a=16),
                        wqr[:, :, m * 128:(m + 1) * 128]), inc=f"w_qc{m % 2}",
                        dma=True)
            if h < NH:
                proj_part(h, WKV, "wkv", n, 1.0, p.tokt("w_k"))
            else:
                proj_part(h, qc[n % 2], "qcol", n, QSCALE, qctoks[n])
            tfree = p.tokt("s_t")
            if h == NH - 1:
                t_wv = emit("gpsimd", lambda e: e.dma_start(
                    WKV.ap().rearrange("p (a n) -> p a n", a=16),
                    wv.rearrange("(a p) n -> p a n", p=128)),
                    waits=[tfree], inc="w_v", dma=True)
            if h >= NH and n + 2 < NH:
                qctoks[n + 2] = emit("gpsimd", lambda e, n=n: e.dma_start(
                    qc[n % 2].ap().rearrange("p (a c) -> p a c", a=16),
                    wqr[:, :, (n + 2) * 128:(n + 3) * 128]),
                    waits=[tfree], inc=f"w_qc{n % 2}", dma=True)
            if h >= 1:
                hp = h - 1
                perm_part(hp, hp % NH, KT_sp if hp < NH else QT_sp)
        perm_part(2 * NH - 1, NH - 1, QT_sp)
        TQV = p.tokt("s_v")
        TQS = p.tokt("s_s")

        vevac = [None, None]
        vspill = [None, None]
        for i in range(NT):
            par = i % 2
            ps = pA if par == 0 else pB
            waits = [p.tokt("w_v"), TQV, TQS]
            if i >= 2:
                waits += [vevac[par], vspill[par]]
            p.phase(BAR0 + waits)
            for db in range(ND):
                for c in range(4):
                    emit("tensor", lambda e, db=db, c=c, ps=ps, i=i: e.matmul(
                        ps[:, c * 512:(c + 1) * 512],
                        XLNT[:, db * 2048 + i * 128: db * 2048 + (i + 1) * 128],
                        WKV[:, db * 2048 + c * 512: db * 2048 + (c + 1) * 512],
                        start=(db == 0), stop=(db == ND - 1)), inc="s_t")
            tmm = p.tokt("s_t")
            te = emit("scalar", lambda e, par=par, ps=ps: e.activation(
                kr[par][:, :], ps[:, :], AF.Copy),
                waits=[tmm], inc="s_s")
            vevac[par] = te
            tsp = emit("sync", lambda e, i=i, par=par: e.dma_start(
                V_sp[i * 128:(i + 1) * 128, :], kr[par][:, :]),
                waits=[te], inc=f"st_v{par}", dma=True)
            vspill[par] = tsp

        p.phase([])
        BAR1 = barrier(extra_st=True)
        for tname, src in (("QT_t", QT_sp), ("KT_t", KT_sp), ("V_t", V_sp)):
            if tname in tap_outs:
                emit("sync", lambda e, tname=tname, src=src: e.dma_start(
                    tap_outs[tname][:, :], src[:, :]), waits=BAR1, inc="st_tap", dma=True)
        if nupto < 2:
            finish(p, emit)
            p.replay(block)
            return nc

        hload = [None] * NH

        def load_head(n, waits):
            par = n % 2
            emit("sync", lambda e, n=n, par=par: e.dma_start(
                qtn[par][:, :], QT_sp[n * 128:(n + 1) * 128, :]),
                waits=waits, inc=f"l_h{par}", dma=True)
            emit("sync", lambda e, n=n, par=par: e.dma_start(
                ktn[par][:, :], KT_sp[n * 128:(n + 1) * 128, :]),
                inc=f"l_h{par}", dma=True)
            return emit("sync", lambda e, n=n, par=par: e.dma_start(
                vn[par][:, :].rearrange("p (a c) -> p a c", a=16),
                V_sp.rearrange("(a p) n -> p a n", p=128)[:, :, n * 128:(n + 1) * 128]),
                inc=f"l_h{par}", dma=True)

        p.phase(BAR1)
        hload[0] = load_head(0, [])
        hload[1] = load_head(1, [])
        wor = wo.rearrange("(a p) n -> p a n", p=128)
        t_spw = emit("gpsimd", lambda e: e.dma_start(
            SPWOQ.ap().rearrange("p (a c) -> p a c", a=16), wor[:, :, 0:512]),
            inc="w_wo0", dma=True)
        def wsrc(w_full, sw_full, pe):
            if pe < 8:
                return w_full.rearrange("E (a p) i -> E p a i", p=128)[pe, :, :, :]
            return sw_full.rearrange("(a p) i -> p a i", p=128)[
                :, :, (pe - 8) * EI:(pe - 7) * EI]

        def w13_load(pe, ib, slot, waits):
            emit("gpsimd", lambda e, pe=pe, ib=ib, slot=slot: e.dma_start(
                w1c[slot].ap().rearrange("p (a c) -> p a c", a=16),
                wsrc(w1, sw1, pe)[:, :, ib * 128:(ib + 1) * 128]),
                waits=waits, inc=f"w_e{slot}", dma=True)
            return emit("gpsimd", lambda e, pe=pe, ib=ib, slot=slot: e.dma_start(
                w3c[slot].ap().rearrange("p (a c) -> p a c", a=16),
                wsrc(w3, sw3, pe)[:, :, ib * 128:(ib + 1) * 128]),
                inc=f"w_e{slot}", dma=True)

        w13_tok0 = w13_load(0, 0, 0, [])
        w13_tok1 = w13_load(0, 1, 1, [])

        piece_free = {0: None, 1: None}
        exp_free = [None, None, None, None]
        z_free = None
        mul_done = [None, None]
        tbcs_done = [None, None]
        pend_z = None
        epar = 0
        epoch = 0
        LA = 3

        def emit_zchain(st):
            nonlocal z_free
            e_, n_, qh_, atps_ = st["e"], st["n"], st["qh"], st["atps"]
            bcps_ = pB if atps_ is pA else pA
            tlnz = emit("scalar", lambda e: e.activation(recf[:, :], pB[0:1, 0:1024],
                                                         AF.Ln),
                        waits=[st["tlast"]], inc="s_s")
            trecb = emit("scalar", lambda e: e.activation(recb[:, :], recf[:, :], AF.Exp,
                                                          scale=-1.0),
                         waits=[tlnz], inc="s_s")
            z_free = tlnz
            for c in range(2):
                emit("tensor", lambda e, c=c, bcps_=bcps_: e.matmul(
                    bcps_[:, 1024 + c * 512:1024 + (c + 1) * 512], ONESR[:, :],
                    recb[0:1, c * 512:(c + 1) * 512], start=True, stop=True),
                    waits=[trecb, mul_done[(e_ + 1) % 2]], inc="s_t")
            tbc = p.tokt("s_t")
            tbcs = emit("scalar", lambda e, bcps_=bcps_: e.activation(
                bcs[:, :], bcps_[:, 1024:2048], AF.Copy),
                waits=[tbc, mul_done[(e_ + 1) % 2]], inc="s_s")
            tbcs_done[e_ % 2] = tbcs
            tm = emit("vector", lambda e, n_=n_, qh_=qh_, atps_=atps_: e.tensor_mul(
                ATTNT[:, n_ * 2048 + qh_ * 1024: n_ * 2048 + (qh_ + 1) * 1024],
                atps_[:, 1024:2048], bcs[:, :]),
                waits=[tbcs], inc="s_v")
            mul_done[e_ % 2] = tm

        for n in range(NH):
            par = n % 2
            p.phase(BAR1 + [hload[n]])
            for qh in range(2):
                atps = pA if epoch % 2 == 0 else pB
                q0 = qh * 1024
                jmax = (qh + 1) * 8
                first_attn = [True, True]
                first_z = [True, True]
                plist = []
                for j in range(jmax):
                    qs_rel = max(0, j * 128 - q0)
                    pieces = []
                    if qs_rel < 512:
                        pieces.append((qs_rel, 512))
                    if qs_rel < 1024:
                        pieces.append((max(qs_rel, 512), 1024))
                    for (a, b) in pieces:
                        if b > a:
                            plist.append((j, a, b, j * 128 >= q0 and a == qs_rel))
                last_in_bank = {}
                for idx, (j, a, b, diag) in enumerate(plist):
                    last_in_bank[a // 512] = idx
                pinfo = {}
                for step in range(len(plist) + LA):
                    if step == 2 and pend_z is not None:
                        emit_zchain(pend_z)
                        pend_z = None
                    if step < len(plist):
                        j, a, b, diag = plist[step]
                        sslot = step % 2
                        sps, soff = pA, sslot * 512
                        ts = emit("tensor", lambda e, par=par, j=j, a=a, b=b, q0=q0,
                                  sps=sps, soff=soff: e.matmul(
                            sps[:, soff:soff + b - a], ktn[par][:, j * 128:(j + 1) * 128],
                            qtn[par][:, q0 + a:q0 + b], start=True, stop=True),
                            waits=[piece_free[sslot]], inc="s_t")
                        my_epar = epar
                        epar = (epar + 1) % 4
                        tex = emit("scalar", lambda e, a=a, b=b, my_epar=my_epar,
                                   sps=sps, soff=soff: e.activation(
                            expt[my_epar][:, 0:b - a], sps[:, soff:soff + b - a], AF.Exp,
                            bias=EXB[:, 0:1]),
                            waits=[ts, exp_free[my_epar]], inc="s_s")
                        piece_free[sslot] = tex
                        gate_tok = tex
                        if diag:
                            gate_tok = emit("vector", lambda e, my_epar=my_epar: e.tensor_mul(
                                expt[my_epar][:, 0:128], expt[my_epar][:, 0:128],
                                TRIUB[:, :]),
                                waits=[tex], inc="s_v")
                        pinfo[step] = (my_epar, gate_tok)
                    dstep = step - LA
                    if 0 <= dstep < len(plist):
                        j, a, b, diag = plist[dstep]
                        bank = a // 512
                        grp_stop = (last_in_bank[bank] == dstep)
                        my_epar, gate_tok = pinfo.pop(dstep)
                        emit("tensor", lambda e, a=a, b=b, my_epar=my_epar,
                             stz=first_z[bank], gsp=grp_stop: e.matmul(
                            pB[0:1, a:b], ONESC[:, :], expt[my_epar][:, 0:b - a],
                            start=stz, stop=gsp),
                            waits=[gate_tok] + ([z_free] if first_z[bank] else []),
                            inc="s_t")
                        first_z[bank] = False
                        tpv = emit("tensor", lambda e, par=par, j=j, a=a, b=b,
                                   my_epar=my_epar, stt=first_attn[bank],
                                   gsp=grp_stop, atps=atps: e.matmul(
                            atps[:, 1024 + a:1024 + b], vn[par][:, j * 128:(j + 1) * 128],
                            expt[my_epar][:, 0:b - a], start=stt, stop=gsp),
                            waits=([tbcs_done[(epoch + 1) % 2]]
                                   if first_attn[bank] else []),
                            inc="s_t")
                        first_attn[bank] = False
                        exp_free[my_epar] = tpv
                pend_z = {"e": epoch, "n": n, "qh": qh, "atps": atps,
                          "tlast": p.tokt("s_t")}
                epoch += 1
            if n + 2 < NH:
                hload[n + 2] = load_head(n + 2, [p.tokt("s_t")])
        emit_zchain(pend_z)
        pend_z = None

        p.phase([])
        BAR2 = barrier()
        if "ATT_t" in tap_outs:
            emit("sync", lambda e: e.dma_start(tap_outs["ATT_t"][:, :], ATTNT[:, :]),
                 waits=BAR2, inc="st_tap", dma=True)
        if nupto < 3:
            finish(p, emit)
            p.replay(block)
            return nc

        p.phase(BAR2)
        wotoks = [t_spw, None, None, None]
        wotoks[1] = emit("gpsimd", lambda e: e.dma_start(
            woq[1].ap().rearrange("p (a c) -> p a c", a=16),
            wor[:, :, 512:1024]), waits=BAR2, inc="w_wo1", dma=True)
        wotoks[2] = emit("gpsimd", lambda e: e.dma_start(
            woq[0].ap().rearrange("p (a c) -> p a c", a=16),
            wor[:, :, 1024:1536]), waits=BAR2, inc="w_wo2", dma=True)

        xt_free = [None, None, None]
        ev_prev = [None, None, None, None]
        sq_free = [None, None]
        gate_free = None
        gt8_free = None
        gsp_prev = None
        tt8_prev = [None, None]
        chain_done = [None, None]
        gsp_toks = []
        pending = None

        def gate_chain(i, par, tgl):
            nonlocal gt8_free, gsp_prev
            GLp = STATS[:, 48 + 8 * par: 56 + 8 * par]
            vchain(lambda e: e.reduce_max(STATS[:, 27:28], GLp, axis=AX.X),
                   extra=[tgl])
            vchain(lambda e, par=par: e.tensor_scalar(
                gwt[par][:, :], GLp, STATS[:, 27:28], None, ALU.is_equal),
                extra=[tt8_prev[par]])
            vchain(lambda e, par=par: e.scalar_tensor_tensor(
                STATS[:, 16:24], gwt[par][:, :], -1e30, GLp, ALU.mult, ALU.add))
            vchain(lambda e: e.reduce_max(STATS[:, 28:29], STATS[:, 16:24], axis=AX.X))
            t_d = vchain(lambda e: e.tensor_sub(STATS[:, 29:30], STATS[:, 28:29],
                                                STATS[:, 27:28]))
            te1 = emit("scalar", lambda e: e.activation(STATS[:, 40:41], STATS[:, 29:30],
                                                        AF.Exp, scale=-1.0),
                       waits=[t_d], inc="s_s")
            te2 = emit("scalar", lambda e: e.activation(STATS[:, 41:42], STATS[:, 29:30],
                                                        AF.Exp), waits=[te1], inc="s_s")
            vchain(lambda e: e.tensor_scalar_add(STATS[:, 42:44], STATS[:, 40:42], 1.0),
                   extra=[te2])
            vchain(lambda e: e.reciprocal(STATS[:, 44:46], STATS[:, 42:44]))
            vchain(lambda e, par=par: e.tensor_scalar(
                STATS[:, 32:40], GLp, STATS[:, 28:29], None, ALU.is_ge))
            vchain(lambda e, par=par: e.tensor_sub(STATS[:, 16:24], STATS[:, 32:40],
                                                   gwt[par][:, :]))
            vchain(lambda e: e.tensor_scalar(STATS[:, 16:24], STATS[:, 16:24],
                                             STATS[:, 44:45], None, ALU.mult))
            vchain(lambda e, par=par: e.tensor_scalar(
                gwt[par][:, :], gwt[par][:, :], STATS[:, 45:46], None, ALU.mult))
            t_w = vchain(lambda e, par=par: e.tensor_add(gwt[par][:, :], gwt[par][:, :],
                                                         STATS[:, 16:24]))
            chain_done[par] = t_w
            tt8 = emit("tensor", lambda e, par=par: e.matmul(
                pB[0:8, 512:640], gwt[par][:, :], IDF[:, :],
                start=True, stop=True, is_transpose=True),
                waits=[t_w, gt8_free], inc="s_t")
            tt8_prev[par] = tt8
            tv8 = emit("vector", lambda e: e.tensor_copy(gt8[:, :], pB[0:8, 512:640]),
                       waits=[tt8, gsp_prev], inc="s_v")
            gt8_free = tv8
            tsp = emit("sync", lambda e, i=i: e.dma_start(
                GSP[:, i * 128:(i + 1) * 128], gt8[:, :]),
                waits=[tv8], inc="st_gsp", dma=True)
            gsp_prev = tsp
            gsp_toks.append(tsp)

        def p3b_finish(pend):
            nonlocal ev_prev, gate_free, pending
            j, jpar, jbpar, jst = pend
            if pending is not None:
                gate_chain(*pending)
                pending = None
            tx = ln_rest(xt[jpar], sqb[jbpar], *jst, LNCOLS[j % 2])
            xt_free[jpar] = tx
            ev_prev, sq_free[jbpar] = emit_transposes(j, ev_prev, tx, sqb[jbpar])
            tev_all = ev_prev[3]
            for db in range(ND):
                emit("tensor", lambda e, db=db, j=j: e.matmul(
                    pB[:, 0:8],
                    XLNT[:, db * 2048 + j * 128: db * 2048 + (j + 1) * 128],
                    GWTB[:, db * 8:(db + 1) * 8],
                    start=(db == 0), stop=(db == ND - 1)),
                    waits=[tev_all, gate_free], inc="s_t")
            tg = p.tokt("s_t")
            tgl = emit("scalar", lambda e, jbpar=jbpar: e.copy(
                STATS[:, 48 + 8 * jbpar: 56 + 8 * jbpar], pB[:, 0:8]),
                waits=[tg, chain_done[jbpar]], inc="s_s")
            gate_free = tgl
            pending = (j, jbpar, tgl)

        psfree = {b: None for b in range(4)}
        xqfree = [None, None, None]
        x2free = [None, None, None]
        p3b_lts = {}
        p3b_st = {}

        def p3b_stats(j):
            p3b_st[j] = ln_stats(xt[j % 3], sqb[j % 2], p3b_lts[j],
                                 [sq_free[j % 2]], LNCOLS[j % 2])

        def p3b_fin(j):
            p3b_finish((j, j % 3, j % 2, p3b_st.pop(j)))

        for dq in range(4):
            wbuf = SPWOQ if dq == 0 else woq[dq % 2]
            p.phase(BAR2 + [wotoks[dq]])
            if dq == 3:
                tok_dq2 = p.tokt("s_t")
                ev_prev = [psfree[0], psfree[2], None, None]
            for i in range(NT):
                par = i % 3
                bank = 2 * (i % 2) + (dq % 2)
                pcol = bank * 512
                lt = emit("sync", lambda e, i=i, dq=dq, par=par: e.dma_start(
                    xq[par][:, :], xin[i * 128:(i + 1) * 128, dq * 512:(dq + 1) * 512]),
                    waits=[xqfree[par]], inc=f"l_xq{par}", dma=True)
                for nb in range(NH):
                    emit("tensor", lambda e, nb=nb, i=i, pcol=pcol, wbuf=wbuf: e.matmul(
                        pA[:, pcol:pcol + 512],
                        ATTNT[:, nb * 2048 + i * 128: nb * 2048 + (i + 1) * 128],
                        wbuf[:, nb * 512:(nb + 1) * 512],
                        start=(nb == 0), stop=(nb == NH - 1)),
                        waits=[psfree[bank]], inc="s_t")
                tmm = p.tokt("s_t")
                tadd = emit("vector", lambda e, par=par, pcol=pcol: e.tensor_add(
                    x2q[par][:, :], xq[par][:, :], pA[:, pcol:pcol + 512]),
                    waits=[tmm, lt, x2free[par]], inc="s_v")
                psfree[bank] = tadd
                xqfree[par] = tadd
                tst = emit("sync", lambda e, i=i, dq=dq, par=par: e.dma_start(
                    X2_sp[i * 128:(i + 1) * 128, dq * 512:(dq + 1) * 512], x2q[par][:, :]),
                    waits=[tadd], inc=f"st_x2{par}", dma=True)
                x2free[par] = tst
                if dq == 3:
                    if i >= 3:
                        p3b_fin(i - 3)
                    p3b_lts[i] = load_xt(
                        i, i % 3, X2_sp,
                        extra=[tst] + ([tok_dq2] if i < 2 else []))
                    if i >= 1:
                        p3b_stats(i - 1)
            if dq == 1:
                wotoks[3] = emit("gpsimd", lambda e: e.dma_start(
                    woq[1].ap().rearrange("p (a c) -> p a c", a=16),
                    wor[:, :, 1536:2048]),
                    waits=[p.tokt("s_t")], inc="w_wo1", dma=True)
        p3b_fin(NT - 3)
        p3b_stats(NT - 1)
        p3b_fin(NT - 2)
        p3b_fin(NT - 1)
        gate_chain(*pending)
        pending = None

        p.phase([])
        BAR3 = barrier(extra_st=True)
        w2r = w2.rearrange("E (a p) d -> E p a d", p=128)
        w2tok00 = emit("gpsimd", lambda e: e.dma_start(
            w2s[0].ap().rearrange("p (a c) -> p a c", a=NIB),
            w2r[0, :, :, 0:1024]), waits=BAR3, inc="w_w20", dma=True)
        if "X2_t" in tap_outs:
            emit("sync", lambda e: e.dma_start(tap_outs["X2_t"][:, :], X2_sp[:, :]),
                 waits=BAR3, inc="st_tap", dma=True)
        if nupto < 4:
            finish(p, emit)
            p.replay(block)
            return nc

        BAR3B = BAR3
        if "XLNT2_t" in tap_outs:
            emit("sync", lambda e: e.dma_start(tap_outs["XLNT2_t"][:, :], XLNT[:, :]),
                 waits=BAR3B, inc="st_tap", dma=True)
        if "GSP_t" in tap_outs:
            emit("sync", lambda e: e.dma_start(tap_outs["GSP_t"][:, :], GSP[:, :]),
                 waits=BAR3B, inc="st_tap", dma=True)
        if nupto < 5:
            finish(p, emit)
            p.replay(block)
            return nc

        p.phase(BAR3B)
        seq = [(pe, ib) for pe in range(NPE) for ib in range(NIB)]
        wtoks = {}
        wtoks[seq[0]] = w13_tok0
        wtoks[seq[1]] = w13_tok1

        h_free = None
        g_free = None
        silu_free = None
        afin_free = [None, None]
        grtok = [None, None]
        grb_free = [None, None]
        beb_use = [None, None]
        tbev_pe = {}

        def grb_load(pe):
            gpar = pe % 2
            grtok[gpar] = emit("gpsimd", lambda e, pe=pe, gpar=gpar: e.dma_start(
                grb[gpar][:, :], GSP[pe:pe + 1, :]),
                waits=[gsp_toks[-1], grb_free[gpar]], inc=f"w_gr{gpar}", dma=True)

        def emit_gate_bcast(pe, waits_extra=(), ps=None):
            gpar = pe % 2
            ps = pA if ps is None else ps
            for c in range(4):
                emit("tensor", lambda e, c=c, gpar=gpar, ps=ps: e.matmul(
                    ps[:, c * 512:(c + 1) * 512], ONESR[:, :],
                    grb[gpar][0:1, c * 512:(c + 1) * 512], start=True, stop=True),
                    waits=[grtok[gpar]] + list(waits_extra), inc="s_t")
            tbe = p.tokt("s_t")
            grb_free[gpar] = tbe
            tb = emit("scalar", lambda e, gpar=gpar, ps=ps: e.activation(
                beb[gpar][:, :], ps[:, :], AF.Copy),
                waits=[tbe, beb_use[gpar]], inc="s_s")
            tbev_pe[pe] = tb
            return tb

        p.phase(BAR3B)
        grb_load(0)
        emit_gate_bcast(0, ps=pB)
        g_free = tbev_pe[0]
        at0_done = None
        pre_asl = None
        pre_x2 = None
        for pe in range(NPE):
            p.phase(BAR3B)
            if pe == 9:
                pre_asl = emit("scalar", lambda e: e.dma_start(
                    aslg[0][:, :].rearrange("p (a c) -> p a c", a=NIB),
                    AT_sp[0, :, :, 0:512].transpose([1, 0, 2])),
                    waits=at0_done, inc="l_asl0", dma=True)
                pre_x2 = emit("sync", lambda e: e.dma_start(
                    x2b[0][:, :], X2_sp[0:128, 0:1024]),
                    waits=[beb_use[1]], inc="l_xq0", dma=True)
            for ib in range(NIB):
                gidx = pe * NIB + ib
                slot = gidx % 2
                for db in range(ND):
                    for c in range(4):
                        emit("tensor", lambda e, db=db, c=c, slot=slot: e.matmul(
                            pA[:, c * 512:(c + 1) * 512],
                            w1c[slot][:, db * 128:(db + 1) * 128],
                            XLNT[:, db * 2048 + c * 512: db * 2048 + (c + 1) * 512],
                            start=(db == 0), stop=(db == ND - 1)),
                            waits=[wtoks[(pe, ib)], h_free], inc="s_t")
                th = p.tokt("s_t")
                tsilu = emit("scalar", lambda e: e.activation(silu[:, :], pA[:, :], AF.Silu),
                             waits=[th, silu_free], inc="s_s")
                h_free = tsilu
                if ib == NIB - 2 and pe + 1 < 8:
                    grb_load(pe + 1)
                if ib == NIB - 1 and pe + 1 < 8:
                    emit_gate_bcast(pe + 1, waits_extra=[tsilu])
                    h_free = tbev_pe[pe + 1]
                for db in range(ND):
                    for c in range(4):
                        emit("tensor", lambda e, db=db, c=c, slot=slot: e.matmul(
                            pB[:, c * 512:(c + 1) * 512],
                            w3c[slot][:, db * 128:(db + 1) * 128],
                            XLNT[:, db * 2048 + c * 512: db * 2048 + (c + 1) * 512],
                            start=(db == 0), stop=(db == ND - 1)),
                            waits=[g_free], inc="s_t")
                tg2 = p.tokt("s_t")
                if gidx + 2 < len(seq):
                    nxt = seq[gidx + 2]
                    wtoks[nxt] = w13_load(nxt[0], nxt[1], (gidx + 2) % 2, [tg2])
                apar = gidx % 2
                if pe < 8:
                    gpar = pe % 2
                    ta = emit("vector", lambda e: e.tensor_mul(atmp[:, :], silu[:, :], pB[:, :]),
                              waits=[tg2, tsilu], inc="s_v")
                    tfin = emit("vector", lambda e, apar=apar, gpar=gpar: e.tensor_mul(
                        afin[apar][:, :], atmp[:, :], beb[gpar][:, :]),
                        waits=[ta, afin_free[apar], tbev_pe[pe]], inc="s_v")
                    beb_use[gpar] = tfin
                else:
                    tfin = emit("vector", lambda e, apar=apar: e.tensor_mul(
                        afin[apar][:, :], silu[:, :], pB[:, :]),
                        waits=[tg2, tsilu, afin_free[apar]], inc="s_v")
                g_free = tfin
                silu_free = tfin
                tsp = emit("sync", lambda e, pe=pe, ib=ib, apar=apar: e.dma_start(
                    AT_sp[pe, ib, :, :], afin[apar][:, :]),
                    waits=[tfin], inc=f"st_af{apar}", dma=True)
                afin_free[apar] = tsp
            if pe == 0:
                at0_done = [p.tokt("st_af0"), p.tokt("st_af1")]

        p.phase([])
        BAR5A = barrier(extra_st=True)
        if "AT_t" in tap_outs:
            emit("sync", lambda e: e.dma_start(tap_outs["AT_t"][:, :, :, :],
                                               AT_sp[:, :, :, :]),
                 waits=BAR5A, inc="st_tap", dma=True)
        if nupto < 6:
            finish(p, emit)
            p.replay(block)
            return nc

        sw2r = sw2.rearrange("(g a p) d -> g p a d", g=2, p=128)

        def w2_load(dh, pe, waits):
            if pe < 8:
                src = w2r[pe, :, :, dh * 1024:(dh + 1) * 1024]
            else:
                src = sw2r[pe - 8, :, :, dh * 1024:(dh + 1) * 1024]
            return emit("gpsimd", lambda e, src=src, pe=pe: e.dma_start(
                w2s[pe % 2].ap().rearrange("p (a c) -> p a c", a=NIB), src),
                waits=waits, inc=f"w_w2{pe % 2}", dma=True)

        yacc_done = [None] * NT
        w2s_free = [None, None]
        aslfree = [None, None]
        ps_free = {0: None, 1: None, 2: None, 3: None}
        x2b_free = [None, None]
        cur_lt = None
        cur_ap2 = 0
        for dh in range(2):
            p.phase(BAR5A)
            prev_out = p.tokt("st_out") if dh == 1 else None
            w2tok = {0: w2tok00 if dh == 0 else w2_load(dh, 0, [w2s_free[0]])}
            x2tok = [None, None]
            w2_consumed = {}
            for pe in range(NPE):
                p.phase(BAR5A + [w2tok[pe]])
                for k in range(NT):
                    kg, kk = k // 4, k % 4
                    apar = (pe * NT + k) % 4
                    psl = [(pA, 0), (pB, 0), (pA, 1024), (pB, 1024)][apar]
                    if pe == 0 and not (dh == 0 and k == 0):
                        x2tok[k % 2] = emit("sync", lambda e, k=k, dh=dh:
                                            e.dma_start(
                            x2b[k % 2][:, :],
                            X2_sp[k * 128:(k + 1) * 128, dh * 1024:(dh + 1) * 1024]),
                            waits=[x2b_free[k % 2]], inc=f"l_xq{k % 2}", dma=True)
                    elif pe == 0:
                        x2tok[0] = pre_x2
                    if kk == 0:
                        cur_ap2 = ((dh * NPE + pe) * 4 + kg) % 2
                        if dh == 0 and pe == 0 and kg == 0:
                            cur_lt = pre_asl
                        else:
                            cur_lt = emit("scalar", lambda e, pe=pe, kg=kg, dh=dh,
                                          ap2=cur_ap2: e.dma_start(
                                aslg[ap2][:, :].rearrange("p (a c) -> p a c", a=NIB),
                                AT_sp[pe, :, :, kg * 512:(kg + 1) * 512].transpose(
                                    [1, 0, 2])),
                                waits=[aslfree[cur_ap2]],
                                inc=f"l_asl{cur_ap2}", dma=True)
                    for ib in range(NIB):
                        for c in range(2):
                            emit("tensor", lambda e, ib=ib, c=c, kk=kk, pe=pe,
                                 ap2=cur_ap2, psl=psl: e.matmul(
                                psl[0][:, psl[1] + c * 512: psl[1] + (c + 1) * 512],
                                aslg[ap2][:, ib * 512 + kk * 128:
                                          ib * 512 + (kk + 1) * 128],
                                w2s[pe % 2][:, ib * 1024 + c * 512:
                                            ib * 1024 + (c + 1) * 512],
                                start=(ib == 0), stop=(ib == NIB - 1)),
                                waits=[cur_lt, ps_free[apar]], inc="s_t")
                    tmm = p.tokt("s_t")
                    if kk == 3:
                        aslfree[cur_ap2] = tmm
                    if pe == 0:
                        tac = emit("vector", lambda e, k=k, psl=psl, kpar=k % 2: e.tensor_add(
                            Yacc[k][:, :], x2b[kpar][:, :],
                            psl[0][:, psl[1]:psl[1] + 1024]),
                            waits=[tmm, x2tok[k % 2], prev_out, yacc_done[k]],
                            inc="s_v")
                        x2b_free[k % 2] = tac
                    else:
                        tac = emit("vector", lambda e, k=k, psl=psl: e.tensor_add(
                            Yacc[k][:, :], Yacc[k][:, :],
                            psl[0][:, psl[1]:psl[1] + 1024]),
                            waits=[tmm, yacc_done[k]], inc="s_v")
                    ps_free[apar] = tac
                    yacc_done[k] = tac
                    if pe == NPE - 1:
                        emit("sync", lambda e, k=k, dh=dh: e.dma_start(
                            out[k * 128:(k + 1) * 128, dh * 1024:(dh + 1) * 1024],
                            Yacc[k][:, :]),
                            waits=[tac], inc="st_out", dma=True)
                w2_consumed[pe] = p.tokt("s_t")
                if pe + 1 < NPE:
                    w2tok[pe + 1] = w2_load(
                        dh, pe + 1,
                        [w2_consumed[pe - 1]] if pe >= 1 else [w2s_free[1]])
            w2s_free = [w2_consumed[8], w2_consumed[NPE - 1]]

        finish(p, emit)
        p.replay(block)
    return nc


def finish(p, emit):
    final = [(s, p.cnt[s]) for s in SEM_NAMES if p.cnt[s] > 0]
    p.phase([])
    emit("sync", lambda e: None, waits=final)


def _make_consts():
    ident = np.eye(128, dtype=np.float32)
    triu = np.triu(np.ones((128, 128), dtype=np.float32))
    perm = np.zeros((128, 128), dtype=np.float32)
    for h in range(128):
        perm[h, h ^ 1] = 1.0
    i = np.arange(64, dtype=np.float64)
    ts = 10000.0 ** (i / 64.0)
    t = np.arange(S, dtype=np.float64)
    ang = t[None, :] / ts[:, None]
    cos = np.cos(ang)
    sin = np.sin(ang)
    cosf = np.repeat(cos, 2, axis=0).astype(np.float32)
    ssin = np.empty((128, S), dtype=np.float32)
    ssin[0::2, :] = -sin
    ssin[1::2, :] = sin
    return ident, triu, perm, cosf, ssin


def _in_maps(kw, x):
    ident, triu, perm, cosf, ssin = _make_consts()
    base = {
        "wq": np.ascontiguousarray(np.asarray(kw["L0_attn_wq"], dtype=np.float32).reshape(D, D)),
        "wk": np.ascontiguousarray(np.asarray(kw["L0_attn_wk"], dtype=np.float32).reshape(D, D)),
        "wv": np.ascontiguousarray(np.asarray(kw["L0_attn_wv"], dtype=np.float32).reshape(D, D)),
        "wo": np.ascontiguousarray(np.asarray(kw["L0_attn_wo"], dtype=np.float32).reshape(D, D)),
        "gwT": np.ascontiguousarray(np.asarray(kw["L0_ffn_gate_w"], dtype=np.float32).T),
        "w1": np.ascontiguousarray(np.asarray(kw["L0_ffn_w1"], dtype=np.float32)),
        "w3": np.ascontiguousarray(np.asarray(kw["L0_ffn_w3"], dtype=np.float32)),
        "w2": np.ascontiguousarray(np.asarray(kw["L0_ffn_w2"], dtype=np.float32)),
        "sw1": np.ascontiguousarray(np.asarray(kw["L0_ffn_sw1"], dtype=np.float32)),
        "sw3": np.ascontiguousarray(np.asarray(kw["L0_ffn_sw3"], dtype=np.float32)),
        "sw2": np.ascontiguousarray(np.asarray(kw["L0_ffn_sw2"], dtype=np.float32)),
        "cident": ident, "ctriu": triu, "cperm": perm,
        "ccos": cosf, "cssin": ssin,
    }
    return [dict(base, xin=np.ascontiguousarray(x[c])) for c in range(8)]


def kernel(L0_attn_x, L0_attn_wq, L0_attn_wk, L0_attn_wv, L0_attn_wo,
           L0_ffn_gate_w, L0_ffn_w1, L0_ffn_w3, L0_ffn_w2,
           L0_ffn_sw1, L0_ffn_sw3, L0_ffn_sw2):
    kw = dict(L0_attn_wq=L0_attn_wq, L0_attn_wk=L0_attn_wk, L0_attn_wv=L0_attn_wv,
              L0_attn_wo=L0_attn_wo, L0_ffn_gate_w=L0_ffn_gate_w, L0_ffn_w1=L0_ffn_w1,
              L0_ffn_w3=L0_ffn_w3, L0_ffn_w2=L0_ffn_w2, L0_ffn_sw1=L0_ffn_sw1,
              L0_ffn_sw3=L0_ffn_sw3, L0_ffn_sw2=L0_ffn_sw2)
    x = np.asarray(L0_attn_x, dtype=np.float32)
    nc = build()
    res = run_bass_kernel_spmd(nc, _in_maps(kw, x), core_ids=list(range(8)))
    return np.stack([res.results[c]["out"] for c in range(8)], axis=0)


# version 27
# speedup vs baseline: 1.0114x; 1.0114x over previous
import contextlib
import math
import numpy as np

import concourse.bass as bass
import concourse.mybir as mybir
from concourse.bass_utils import run_bass_kernel_spmd

F32 = mybir.dt.float32
BF16 = mybir.dt.bfloat16
AF = mybir.ActivationFunctionType
ALU = mybir.AluOpType
AX = mybir.AxisListType

S = 2048
D = 2048
NT = 16
ND = 16
NH = 16
EI = 1408
NIB = 11
NPE = 10
QSCALE = 1.0 / math.sqrt(128.0)
EXPBIAS = -12.0
EPS = 1e-5

ENGINES = ("tensor", "vector", "scalar", "gpsimd", "sync")

RESV = 16512
SMALL = RESV
WB = SMALL + 2048
XLNT_OFF = WB + 45056
R1_OFF = XLNT_OFF + 65536
SPARE = R1_OFF + 65536

PHASES = ("p0", "p1", "p2", "p3", "p3b", "p5a", "p5b")

SEM_NAMES = (
    ["s_t", "s_v", "s_s", "s_g"]
    + ["w_c", "w_cs", "w_k", "w_v", "w_qc0", "w_qc1", "w_wo0", "w_wo1",
       "w_e0", "w_e1", "w_gr0", "w_gr1", "w_w20", "w_w21"]
    + ["l_xt0", "l_xt1", "l_xt2", "l_xs0", "l_xs1", "l_xs2",
       "l_xq0", "l_xq1", "l_xq2", "w_wo2", "l_h0", "l_h1",
       "l_asl0", "l_asl1", "l_asl2", "l_asl3", "l_yin"]
    + ["st_kq0", "st_kq1", "st_v0", "st_v1", "st_x20", "st_x21", "st_x22",
       "st_gsp", "st_af0", "st_af1", "st_out", "st_tap"]
)
DMA_SEMS = [s for s in SEM_NAMES
            if (s[0] in "wl" and s != "s_g") or s.startswith("st")]


class Prog:
    def __init__(self, nc):
        self.nc = nc
        self.ops = {e: [] for e in ENGINES}
        self.semh = {}
        self.cnt = {}
        self.phase_waits = []

    def add_sem(self, name, handle):
        self.semh[name] = handle
        self.cnt[name] = 0

    def phase(self, waits):
        self.phase_waits = [w for w in waits if w is not None]

    def emit(self, eng, fn, waits=(), inc=None, dma=False):
        amt = 16 if dma else 1
        tok = None
        incs = []
        if inc is not None:
            self.cnt[inc] += amt
            tok = (inc, self.cnt[inc])
            incs.append((inc, amt))
        w = self.phase_waits + [x for x in waits if x is not None]
        self.ops[eng].append([fn, w, incs])
        return tok

    def mark(self, eng, sem):
        assert self.ops[eng], f"no ops on {eng} to mark"
        rec = self.ops[eng][-1]
        for (s, _a) in rec[2]:
            if s == sem:
                return (sem, self.cnt[sem])
        self.cnt[sem] += 1
        rec[2].append((sem, 1))
        return (sem, self.cnt[sem])

    def tokt(self, sem):
        return (sem, self.cnt[sem])

    def all_dma_toks(self):
        return [(s, self.cnt[s]) for s in DMA_SEMS if self.cnt[s] > 0]

    def replay(self, block):
        for eng_name in ENGINES:
            section = getattr(block, eng_name)
            oplist = self.ops[eng_name]
            semh = self.semh

            def make(oplist=oplist):
                lastwait = {}

                def _(eng):
                    for fn, waits, incs in oplist:
                        for (s, v) in waits:
                            if v > 0 and lastwait.get(s, -1) < v:
                                eng.wait_ge(semh[s], v)
                                lastwait[s] = v
                        inst = fn(eng)
                        if inst is None:
                            assert not incs
                            continue
                        for (s, amt) in incs:
                            r = inst.then_inc(semh[s], amt)
                            if r is not None:
                                inst = r
                return _

            section(make())


def build(taps=(), upto="p5b"):
    nupto = PHASES.index(upto)
    nc = bass.Bass("TRN2", target_bir_lowering=False)
    assert nc.sbuf_base <= SMALL, f"bass internal allocs reached {nc.sbuf_base}"

    xin = nc.dram_tensor("xin", [S, D], F32, kind="ExternalInput")
    wq = nc.dram_tensor("wq", [D, D], F32, kind="ExternalInput")
    wk = nc.dram_tensor("wk", [D, D], F32, kind="ExternalInput")
    wv = nc.dram_tensor("wv", [D, D], F32, kind="ExternalInput")
    wo = nc.dram_tensor("wo", [D, D], F32, kind="ExternalInput")
    gwT = nc.dram_tensor("gwT", [D, 8], F32, kind="ExternalInput")
    w1 = nc.dram_tensor("w1", [8, D, EI], F32, kind="ExternalInput")
    w3 = nc.dram_tensor("w3", [8, D, EI], F32, kind="ExternalInput")
    w2 = nc.dram_tensor("w2", [8, EI, D], F32, kind="ExternalInput")
    sw1 = nc.dram_tensor("sw1", [D, 2 * EI], F32, kind="ExternalInput")
    sw3 = nc.dram_tensor("sw3", [D, 2 * EI], F32, kind="ExternalInput")
    sw2 = nc.dram_tensor("sw2", [2 * EI, D], F32, kind="ExternalInput")
    cident = nc.dram_tensor("cident", [128, 128], F32, kind="ExternalInput")
    ctriu = nc.dram_tensor("ctriu", [128, 128], F32, kind="ExternalInput")
    cperm = nc.dram_tensor("cperm", [128, 128], F32, kind="ExternalInput")
    ccos = nc.dram_tensor("ccos", [128, S], F32, kind="ExternalInput")
    cssin = nc.dram_tensor("cssin", [128, S], F32, kind="ExternalInput")
    out = nc.dram_tensor("out", [S, D], F32, kind="ExternalOutput")

    QT_sp = nc.dram_tensor("QT_sp", [D, S], BF16, kind="Internal")
    KT_sp = nc.dram_tensor("KT_sp", [D, S], BF16, kind="Internal")
    V_sp = nc.dram_tensor("V_sp", [S, D], BF16, kind="Internal")
    X2_sp = nc.dram_tensor("X2_sp", [S, D], F32, kind="Internal")
    AT_sp = nc.dram_tensor("AT_sp", [NPE, NIB, 128, S], BF16, kind="Internal")
    GSP = nc.dram_tensor("GSP", [8, S], F32, kind="Internal")

    tap_outs = {}
    tap_specs = {"XLNT_t": ([128, 16 * 2048], BF16), "QT_t": ([D, S], BF16),
                 "KT_t": ([D, S], BF16), "V_t": ([S, D], BF16),
                 "ATT_t": ([128, 16 * 2048], BF16), "X2_t": ([S, D], F32),
                 "GSP_t": ([8, S], F32), "STATS_t": ([128, 48], F32),
                 "XLNT2_t": ([128, 16 * 2048], BF16),
                 "AT_t": ([NPE, NIB, 128, S], BF16)}
    for t in taps:
        shp, dt = tap_specs[t]
        tap_outs[t] = nc.dram_tensor(t, shp, dt, kind="ExternalOutput")

    A = nc.alloc_sbuf_tensor_at
    IDF = A("IDF", [128, 128], F32, offset=SMALL + 0)
    TRIUB = A("TRIUB", [128, 128], BF16, offset=SMALL + 512)
    PERMB = A("PERMB", [128, 128], BF16, offset=SMALL + 768)
    ONESC = A("ONESC", [128, 1], BF16, offset=SMALL + 1024)
    ONESR = A("ONESR", [1, 128], BF16, offset=SMALL + 1056)
    GWTB = A("GWTB", [128, 128], BF16, offset=SMALL + 1312)
    STATS = A("STATS", [128, 64], F32, offset=SMALL + 1568)
    EXB = A("EXB", [128, 1], F32, offset=SMALL + 1824)

    xt = [A("xt0", [128, D], F32, offset=WB + 0),
          A("xt1", [128, D], F32, offset=WB + 8192),
          A("xt2", [128, D], F32, offset=SPARE + 8192)]
    sqb = [A("sqb0", [128, D], BF16, offset=SPARE + 0),
           A("sqb1", [128, D], BF16, offset=SPARE + 4096)]
    gwt = [A("gwt0", [128, 8], F32, offset=SPARE + 33024),
           A("gwt1", [128, 8], F32, offset=SPARE + 33056)]
    gt8 = A("gt8", [8, 128], F32, offset=SPARE + 33088)
    IDB = A("IDB", [128, 128], BF16, offset=SPARE + 32768)
    SPWOQ = A("SPWOQ", [128, 16 * 512], BF16, offset=SPARE + 0)

    kr = [A("kr0", [128, S], BF16, offset=WB + 0),
          A("kr1", [128, S], BF16, offset=WB + 4096)]
    t1b = A("t1b", [128, S], BF16, offset=WB + 8192)
    t2b = A("t2b", [128, S], BF16, offset=WB + 12288)
    krp = [A("krp0", [128, S], BF16, offset=WB + 16384),
           A("krp1", [128, S], BF16, offset=WB + 20480)]
    qc = [A("qc0", [128, S], BF16, offset=WB + 24576),
          A("qc1", [128, S], BF16, offset=WB + 28672)]
    COSB = A("COSB", [128, S], BF16, offset=WB + 32768)
    SSINB = A("SSINB", [128, S], BF16, offset=WB + 36864)

    qtn = [A("qtn0", [128, S], BF16, offset=WB + 0),
           A("qtn1", [128, S], BF16, offset=WB + 4096)]
    ktn = [A("ktn0", [128, S], BF16, offset=WB + 8192),
           A("ktn1", [128, S], BF16, offset=WB + 12288)]
    vn = [A("vn0", [128, S], BF16, offset=WB + 16384),
          A("vn1", [128, S], BF16, offset=WB + 20480)]
    expt = [A("exp0", [128, 512], BF16, offset=WB + 24576),
            A("exp1", [128, 512], BF16, offset=WB + 25600),
            A("exp2", [128, 512], BF16, offset=WB + 41984),
            A("exp3", [128, 512], BF16, offset=WB + 43008)]
    bcs = A("bcs", [128, 1024], BF16, offset=WB + 26624)
    zsb = A("zsb", [1, 1024], F32, offset=WB + 28672)
    recf = A("recf", [1, 1024], F32, offset=WB + 32768)
    recb = A("recb", [1, 1024], BF16, offset=WB + 36864)

    woq = [A("woq0", [128, 16 * 512], BF16, offset=WB + 0),
           A("woq1", [128, 16 * 512], BF16, offset=WB + 16384)]
    xq = [A("xq0", [128, 512], F32, offset=WB + 32768),
          A("xq1", [128, 512], F32, offset=WB + 34816),
          A("xq2", [128, 512], F32, offset=WB + 36864)]
    x2q = [A("x2q0", [128, 512], F32, offset=WB + 38912),
           A("x2q1", [128, 512], F32, offset=WB + 40960),
           A("x2q2", [128, 512], F32, offset=WB + 43008)]

    w1c = [A("w1c0", [128, S], BF16, offset=SPARE + 16384),
           A("w1c1", [128, S], BF16, offset=SPARE + 24576)]
    w3c = [A("w3c0", [128, S], BF16, offset=SPARE + 20480),
           A("w3c1", [128, S], BF16, offset=SPARE + 28672)]
    silu = A("silu", [128, S], BF16, offset=WB + 16384)
    atmp = A("atmp", [128, S], BF16, offset=WB + 20480)
    afin = [A("afin0", [128, S], BF16, offset=WB + 24576),
            A("afin1", [128, S], BF16, offset=WB + 28672)]
    beb = [A("beb0", [128, S], BF16, offset=WB + 32768),
           A("beb1", [128, S], BF16, offset=WB + 0)]
    grb = [A("gr0", [1, S], BF16, offset=WB + 36864),
           A("gr1", [1, S], BF16, offset=WB + 40960)]

    aslg = [A("aslg0", [128, NIB * 512], BF16, offset=WB + 4096),
            A("aslg1", [128, NIB * 512], BF16, offset=WB + 15360)]
    x2b = [A("x2b0", [128, 1024], F32, offset=WB + 0),
           A("x2b1", [128, 1024], F32, offset=WB + 26624)]

    XLNT = A("XLNT", [128, 16 * 2048], BF16, offset=XLNT_OFF)
    Yacc = [A(f"Yacc{k}", [128, 1024], F32, offset=XLNT_OFF + 4096 * k)
            for k in range(NT)]
    WKV = A("WKV", [128, 16 * 2048], BF16, offset=R1_OFF)
    ATTNT = A("ATTNT", [128, 16 * 2048], BF16, offset=R1_OFF)
    w2s = [A("w2s0", [128, NIB * 1024], BF16, offset=R1_OFF + 0),
           A("w2s1", [128, NIB * 1024], BF16, offset=R1_OFF + 22528)]

    pA = nc.alloc_psum_tensor("pA", [128, 2048], F32)
    pB = nc.alloc_psum_tensor("pB", [128, 2048], F32)

    with contextlib.ExitStack() as stack:
        p = Prog(nc)
        for name in SEM_NAMES:
            p.add_sem(name, stack.enter_context(nc.semaphore(name)))
        block = stack.enter_context(nc.Block())
        emit = p.emit
        mark = p.mark

        def barrier(extra_st=False):
            w = [mark("tensor", "s_t"), mark("vector", "s_v"), mark("scalar", "s_s")]
            if extra_st:
                w += p.all_dma_toks()
            return w

        p.phase([])
        t_consts = None
        for (dstt, srct) in ((IDF, cident), (TRIUB, ctriu), (PERMB, cperm)):
            t_consts = emit("gpsimd",
                            lambda e, d=dstt, s=srct: e.dma_start(d[:, :], s[:, :]),
                            inc="w_c", dma=True)
        gsrc = gwT.rearrange("(a p) e -> p a e", p=128)
        t_consts = emit("gpsimd",
                        lambda e: e.dma_start(
                            GWTB.ap().rearrange("p (a c) -> p a c", a=16)[:, :, 0:8],
                            gsrc),
                        inc="w_c", dma=True)
        t_wk = emit("gpsimd",
                    lambda e: e.dma_start(
                        WKV.ap().rearrange("p (a n) -> p a n", a=16),
                        wk.rearrange("(a p) n -> p a n", p=128)),
                    inc="w_k", dma=True)
        emit("vector", lambda e: e.memset(ONESC[:, :], 1.0))
        emit("vector", lambda e: e.memset(ONESR[:, :], 1.0))
        emit("vector", lambda e: e.memset(EXB[:, :], EXPBIAS))
        t_ones = mark("vector", "s_v")
        t_idb = emit("scalar", lambda e: e.activation(IDB[:, :], IDF[:, :], AF.Copy),
                     waits=[p.tokt("w_c")], inc="s_s")

        def vchain(fn, extra=()):
            return emit("vector", fn, waits=[p.tokt("s_v")] + list(extra), inc="s_v")

        LNCOLS = [[0, 1, 2, 3, 4, 6, 7, 8], [9, 10, 11, 12, 13, 14, 15, 24]]

        def c1(col):
            return STATS[:, col:col + 1]

        def ln_stats(xtile, dstb, src_toks, sq_waits, C):
            tsq = emit("scalar", lambda e: e.activation(dstb[:, :], xtile[:, :], AF.Square,
                                                        accum_out=c1(C[1])),
                       waits=list(src_toks) + list(sq_waits), inc="s_s")
            tsum = vchain(lambda e: e.reduce_sum(c1(C[0]), xtile[:, :], axis=AX.X),
                          extra=list(src_toks))
            return tsq, tsum

        def ln_rest(xtile, dstb, tsq, tsum, C):
            vchain(lambda e: e.tensor_scalar_mul(c1(C[2]), c1(C[0]), 1.0 / D),
                   extra=[tsum])
            vchain(lambda e: e.tensor_scalar(c1(C[3]), c1(C[1]), 1.0 / D, EPS,
                                             ALU.mult, ALU.add), extra=[tsq])
            vchain(lambda e: e.tensor_mul(c1(C[4]), c1(C[2]), c1(C[2])))
            tv = vchain(lambda e: e.tensor_sub(c1(C[5]), c1(C[3]), c1(C[4])))
            tln = emit("scalar", lambda e: e.activation(c1(C[6]), c1(C[5]), AF.Ln),
                       waits=[tv], inc="s_s")
            tr = emit("scalar", lambda e: e.activation(c1(C[7]), c1(C[6]), AF.Exp,
                                                       scale=-0.5), waits=[tln], inc="s_s")
            tnb = vchain(lambda e: e.scalar_tensor_tensor(c1(C[4]), c1(C[2]), -1.0,
                                                          c1(C[7]), ALU.mult, ALU.mult),
                         extra=[tr])
            emit("scalar", lambda e: e.activation(dstb[:, :], xtile[:, :], AF.Identity,
                                                  bias=c1(C[4]), scale=c1(C[7])),
                 waits=[tnb], inc="s_s")
            return p.tokt("s_s")

        def load_xt(i, par, srcten, extra=()):
            t1 = emit("sync", lambda e, i=i, par=par, srcten=srcten: e.dma_start(
                xt[par][:, :], srcten[i * 128:(i + 1) * 128, :]),
                waits=[xt_free[par]] + list(extra), inc=f"l_xt{par}", dma=True)
            return [t1]

        pAb = pA.ap().bitcast(BF16)
        pBb = pB.ap().bitcast(BF16)

        def emit_transposes(i, ev_prev, sink, sq):
            ev = [None, None, None, None]
            gco = (0, 2048, 2048, 3072)
            for g in range(4):
                psb = pAb if g < 2 else pBb
                coff = gco[g]
                for jj in range(4):
                    j = g * 4 + jj
                    emit("tensor", lambda e, psb=psb, coff=coff, jj=jj, j=j, sq=sq: e.matmul(
                        psb[:, coff + jj * 128: coff + (jj + 1) * 128],
                        sq[:, j * 128:(j + 1) * 128], IDB[:, :],
                        start=True, stop=True, is_transpose=True),
                        waits=[sink, ev_prev[g]], inc="s_t")
                tt = p.tokt("s_t")
                jbase = g * 4

                def evac(e, psb=psb, coff=coff, jbase=jbase, i=i):
                    dst = XLNT.ap().rearrange("p (a c) -> p a c", a=16)[
                        :, jbase:jbase + 4, i * 128:(i + 1) * 128]
                    return e.copy(dst, psb[:, coff:coff + 512].rearrange(
                        "p (a c) -> p a c", a=4))
                ev[g] = emit("scalar", evac, waits=[tt], inc="s_s")
            return ev, p.tokt("s_t")

        p.phase([p.tokt("w_c"), t_ones, t_idb])
        xt_free = [None, None, None]
        ev_prev = [None, None, None, None]
        sq_free = [None, None]
        pend_ln = None

        def ln_finish(pend):
            nonlocal ev_prev
            j, jpar, jbpar, jst = pend
            tx = ln_rest(xt[jpar], sqb[jbpar], *jst, LNCOLS[j % 2])
            xt_free[jpar] = tx
            ev_prev, sq_free[jbpar] = emit_transposes(j, ev_prev, tx, sqb[jbpar])

        lts_q = {0: load_xt(0, 0, xin), 1: load_xt(1, 1, xin)}
        for i in range(NT):
            par = i % 3
            bpar = i % 2
            if pend_ln is not None:
                ln_finish(pend_ln)
            if i + 2 < NT:
                lts_q[i + 2] = load_xt(i + 2, (i + 2) % 3, xin)
            st = ln_stats(xt[par], sqb[bpar], lts_q[i], [sq_free[bpar]], LNCOLS[i % 2])
            pend_ln = (i, par, bpar, st)
        ln_finish(pend_ln)
        pend_ln = None
        BAR0 = barrier()
        emit("gpsimd", lambda e: e.dma_start(COSB[:, :], ccos[:, :]),
             waits=BAR0, inc="w_cs", dma=True)
        cos_tok = emit("gpsimd", lambda e: e.dma_start(SSINB[:, :], cssin[:, :]),
                       inc="w_cs", dma=True)
        if "STATS_t" in tap_outs:
            emit("sync", lambda e: e.dma_start(tap_outs["STATS_t"][:, :], STATS[:, :]),
                 waits=BAR0, inc="st_tap", dma=True)
        if "XLNT_t" in tap_outs:
            emit("sync", lambda e: e.dma_start(tap_outs["XLNT_t"][:, :], XLNT[:, :]),
                 waits=BAR0, inc="st_tap", dma=True)
        if nupto < 1:
            finish(p, emit)
            p.replay(block)
            return nc

        wqr = wq.rearrange("(a p) n -> p a n", p=128)
        qctoks = [None] * NH
        TE = {}; TV2 = {}; TROPE = {}; TSP = {}; PSH = {}

        def proj_part(h, lhs_tile, lhs_kind, n, scale, wtok):
            par = h % 2
            ps = pA if par == 0 else pB
            for db in range(ND):
                lcol = db * 2048 + n * 128 if lhs_kind == "wkv" else db * 128
                for c in range(4):
                    emit("tensor", lambda e, db=db, c=c, lcol=lcol, ps=ps,
                         lhs_tile=lhs_tile: e.matmul(
                        ps[:, c * 512:(c + 1) * 512],
                        lhs_tile[:, lcol:lcol + 128],
                        XLNT[:, db * 2048 + c * 512: db * 2048 + (c + 1) * 512],
                        start=(db == 0), stop=(db == ND - 1)),
                        waits=[wtok], inc="s_t")
            tproj = p.tokt("s_t")
            TE[h] = emit("scalar", lambda e, par=par, scale=scale, ps=ps: e.activation(
                kr[par][:, :], ps[:, :], AF.Copy, scale=scale),
                waits=[tproj], inc="s_s")
            PSH[h] = ps

        def perm_part(h, n, spill):
            par = h % 2
            ps = PSH[h]
            for c in range(4):
                emit("tensor", lambda e, c=c, par=par, ps=ps: e.matmul(
                    ps[:, c * 512:(c + 1) * 512], PERMB[:, :],
                    kr[par][:, c * 512:(c + 1) * 512], start=True, stop=True),
                    waits=[TE[h]], inc="s_t")
            tperm = p.tokt("s_t")
            t1t = emit("vector", lambda e, par=par: e.tensor_mul(t1b[:, :], kr[par][:, :],
                                                                 COSB[:, :]),
                       waits=[TE[h], TROPE.get(h - 1)], inc="s_v")
            tv2 = emit("vector", lambda e, ps=ps: e.tensor_mul(t2b[:, :], ps[:, :],
                                                               SSINB[:, :]),
                       waits=[tperm], inc="s_v")
            trope = emit("vector", lambda e, par=par: e.tensor_add(krp[par][:, :], t1b[:, :],
                                                                   t2b[:, :]),
                         waits=[t1t, tv2, TSP.get(h - 2)], inc="s_v")
            tsp = emit("sync", lambda e, n=n, par=par, spill=spill: e.dma_start(
                spill[n * 128:(n + 1) * 128, :], krp[par][:, :]),
                waits=[trope], inc=f"st_kq{par}", dma=True)
            TV2[h] = tv2
            TROPE[h] = trope
            TSP[h] = tsp

        t_wv = None
        for h in range(2 * NH):
            n = h % NH
            p.phase(BAR0 + [cos_tok] + [t for t in [TV2.get(h - 2)] if t])
            if h == 0:
                for m in range(2):
                    qctoks[m] = emit("gpsimd", lambda e, m=m: e.dma_start(
                        qc[m % 2].ap().rearrange("p (a c) -> p a c", a=16),
                        wqr[:, :, m * 128:(m + 1) * 128]), inc=f"w_qc{m % 2}",
                        dma=True)
            if h < NH:
                proj_part(h, WKV, "wkv", n, 1.0, p.tokt("w_k"))
            else:
                proj_part(h, qc[n % 2], "qcol", n, QSCALE, qctoks[n])
            tfree = p.tokt("s_t")
            if h == NH - 1:
                t_wv = emit("gpsimd", lambda e: e.dma_start(
                    WKV.ap().rearrange("p (a n) -> p a n", a=16),
                    wv.rearrange("(a p) n -> p a n", p=128)),
                    waits=[tfree], inc="w_v", dma=True)
            if h >= NH and n + 2 < NH:
                qctoks[n + 2] = emit("gpsimd", lambda e, n=n: e.dma_start(
                    qc[n % 2].ap().rearrange("p (a c) -> p a c", a=16),
                    wqr[:, :, (n + 2) * 128:(n + 3) * 128]),
                    waits=[tfree], inc=f"w_qc{n % 2}", dma=True)
            if h >= 1:
                hp = h - 1
                perm_part(hp, hp % NH, KT_sp if hp < NH else QT_sp)
        perm_part(2 * NH - 1, NH - 1, QT_sp)
        TQV = p.tokt("s_v")
        TQS = p.tokt("s_s")

        vevac = [None, None]
        vspill = [None, None]
        for i in range(NT):
            par = i % 2
            ps = pA if par == 0 else pB
            waits = [p.tokt("w_v"), TQV, TQS]
            if i >= 2:
                waits += [vevac[par], vspill[par]]
            p.phase(BAR0 + waits)
            for db in range(ND):
                for c in range(4):
                    emit("tensor", lambda e, db=db, c=c, ps=ps, i=i: e.matmul(
                        ps[:, c * 512:(c + 1) * 512],
                        XLNT[:, db * 2048 + i * 128: db * 2048 + (i + 1) * 128],
                        WKV[:, db * 2048 + c * 512: db * 2048 + (c + 1) * 512],
                        start=(db == 0), stop=(db == ND - 1)), inc="s_t")
            tmm = p.tokt("s_t")
            te = emit("scalar", lambda e, par=par, ps=ps: e.activation(
                kr[par][:, :], ps[:, :], AF.Copy),
                waits=[tmm], inc="s_s")
            vevac[par] = te
            tsp = emit("sync", lambda e, i=i, par=par: e.dma_start(
                V_sp[i * 128:(i + 1) * 128, :], kr[par][:, :]),
                waits=[te], inc=f"st_v{par}", dma=True)
            vspill[par] = tsp

        p.phase([])
        BAR1 = barrier(extra_st=True)
        for tname, src in (("QT_t", QT_sp), ("KT_t", KT_sp), ("V_t", V_sp)):
            if tname in tap_outs:
                emit("sync", lambda e, tname=tname, src=src: e.dma_start(
                    tap_outs[tname][:, :], src[:, :]), waits=BAR1, inc="st_tap", dma=True)
        if nupto < 2:
            finish(p, emit)
            p.replay(block)
            return nc

        hload = [None] * NH

        def load_head(n, waits):
            par = n % 2
            emit("sync", lambda e, n=n, par=par: e.dma_start(
                qtn[par][:, :], QT_sp[n * 128:(n + 1) * 128, :]),
                waits=waits, inc=f"l_h{par}", dma=True)
            emit("sync", lambda e, n=n, par=par: e.dma_start(
                ktn[par][:, :], KT_sp[n * 128:(n + 1) * 128, :]),
                inc=f"l_h{par}", dma=True)
            return emit("sync", lambda e, n=n, par=par: e.dma_start(
                vn[par][:, :].rearrange("p (a c) -> p a c", a=16),
                V_sp.rearrange("(a p) n -> p a n", p=128)[:, :, n * 128:(n + 1) * 128]),
                inc=f"l_h{par}", dma=True)

        p.phase(BAR1)
        hload[0] = load_head(0, [])
        hload[1] = load_head(1, [])
        wor = wo.rearrange("(a p) n -> p a n", p=128)
        t_spw = emit("gpsimd", lambda e: e.dma_start(
            SPWOQ.ap().rearrange("p (a c) -> p a c", a=16), wor[:, :, 0:512]),
            inc="w_wo0", dma=True)
        def wsrc(w_full, sw_full, pe):
            if pe < 8:
                return w_full.rearrange("E (a p) i -> E p a i", p=128)[pe, :, :, :]
            return sw_full.rearrange("(a p) i -> p a i", p=128)[
                :, :, (pe - 8) * EI:(pe - 7) * EI]

        def w13_load(pe, ib, slot, waits):
            emit("gpsimd", lambda e, pe=pe, ib=ib, slot=slot: e.dma_start(
                w1c[slot].ap().rearrange("p (a c) -> p a c", a=16),
                wsrc(w1, sw1, pe)[:, :, ib * 128:(ib + 1) * 128]),
                waits=waits, inc=f"w_e{slot}", dma=True)
            return emit("gpsimd", lambda e, pe=pe, ib=ib, slot=slot: e.dma_start(
                w3c[slot].ap().rearrange("p (a c) -> p a c", a=16),
                wsrc(w3, sw3, pe)[:, :, ib * 128:(ib + 1) * 128]),
                inc=f"w_e{slot}", dma=True)

        w13_tok0 = w13_load(0, 0, 0, [])
        w13_tok1 = w13_load(0, 1, 1, [])

        piece_free = {0: None, 1: None}
        exp_free = [None, None, None, None]
        z_free = None
        mul_done = [None, None]
        tbcs_done = [None, None]
        pend_z = None
        epar = 0
        epoch = 0
        LA = 3

        def emit_zchain(st):
            nonlocal z_free
            e_, n_, qh_, atps_ = st["e"], st["n"], st["qh"], st["atps"]
            bcps_ = pB if atps_ is pA else pA
            tlnz = emit("scalar", lambda e: e.activation(recf[:, :], pB[0:1, 0:1024],
                                                         AF.Ln),
                        waits=[st["tlast"]], inc="s_s")
            trecb = emit("scalar", lambda e: e.activation(recb[:, :], recf[:, :], AF.Exp,
                                                          scale=-1.0),
                         waits=[tlnz], inc="s_s")
            z_free = tlnz
            for c in range(2):
                emit("tensor", lambda e, c=c, bcps_=bcps_: e.matmul(
                    bcps_[:, 1024 + c * 512:1024 + (c + 1) * 512], ONESR[:, :],
                    recb[0:1, c * 512:(c + 1) * 512], start=True, stop=True),
                    waits=[trecb, mul_done[(e_ + 1) % 2]], inc="s_t")
            tbc = p.tokt("s_t")
            tbcs = emit("vector", lambda e, bcps_=bcps_: e.tensor_copy(
                bcs[:, :], bcps_[:, 1024:2048]),
                waits=[tbc, mul_done[(e_ + 1) % 2]], inc="s_v")
            tbcs_done[e_ % 2] = tbcs
            tm = emit("vector", lambda e, n_=n_, qh_=qh_, atps_=atps_: e.tensor_mul(
                ATTNT[:, n_ * 2048 + qh_ * 1024: n_ * 2048 + (qh_ + 1) * 1024],
                atps_[:, 1024:2048], bcs[:, :]),
                waits=[tbcs], inc="s_v")
            mul_done[e_ % 2] = tm

        for n in range(NH):
            par = n % 2
            p.phase(BAR1 + [hload[n]])
            for qh in range(2):
                atps = pA if epoch % 2 == 0 else pB
                q0 = qh * 1024
                jmax = (qh + 1) * 8
                first_attn = [True, True]
                first_z = [True, True]
                plist = []
                for j in range(jmax):
                    qs_rel = max(0, j * 128 - q0)
                    pieces = []
                    if qs_rel < 512:
                        pieces.append((qs_rel, 512))
                    if qs_rel < 1024:
                        pieces.append((max(qs_rel, 512), 1024))
                    for (a, b) in pieces:
                        if b > a:
                            plist.append((j, a, b, j * 128 >= q0 and a == qs_rel))
                last_in_bank = {}
                for idx, (j, a, b, diag) in enumerate(plist):
                    last_in_bank[a // 512] = idx
                pinfo = {}
                for step in range(len(plist) + LA):
                    if step == 2 and pend_z is not None:
                        emit_zchain(pend_z)
                        pend_z = None
                    if step < len(plist):
                        j, a, b, diag = plist[step]
                        sslot = step % 2
                        sps, soff = pA, sslot * 512
                        ts = emit("tensor", lambda e, par=par, j=j, a=a, b=b, q0=q0,
                                  sps=sps, soff=soff: e.matmul(
                            sps[:, soff:soff + b - a], ktn[par][:, j * 128:(j + 1) * 128],
                            qtn[par][:, q0 + a:q0 + b], start=True, stop=True),
                            waits=[piece_free[sslot]], inc="s_t")
                        my_epar = epar
                        epar = (epar + 1) % 4
                        tex = emit("scalar", lambda e, a=a, b=b, my_epar=my_epar,
                                   sps=sps, soff=soff: e.activation(
                            expt[my_epar][:, 0:b - a], sps[:, soff:soff + b - a], AF.Exp,
                            bias=EXB[:, 0:1]),
                            waits=[ts, exp_free[my_epar]], inc="s_s")
                        piece_free[sslot] = tex
                        gate_tok = tex
                        if diag:
                            gate_tok = emit("vector", lambda e, my_epar=my_epar: e.tensor_mul(
                                expt[my_epar][:, 0:128], expt[my_epar][:, 0:128],
                                TRIUB[:, :]),
                                waits=[tex], inc="s_v")
                        pinfo[step] = (my_epar, gate_tok)
                    dstep = step - LA
                    if 0 <= dstep < len(plist):
                        j, a, b, diag = plist[dstep]
                        bank = a // 512
                        grp_stop = (last_in_bank[bank] == dstep)
                        my_epar, gate_tok = pinfo.pop(dstep)
                        emit("tensor", lambda e, a=a, b=b, my_epar=my_epar,
                             stz=first_z[bank], gsp=grp_stop: e.matmul(
                            pB[0:1, a:b], ONESC[:, :], expt[my_epar][:, 0:b - a],
                            start=stz, stop=gsp),
                            waits=[gate_tok] + ([z_free] if first_z[bank] else []),
                            inc="s_t")
                        first_z[bank] = False
                        tpv = emit("tensor", lambda e, par=par, j=j, a=a, b=b,
                                   my_epar=my_epar, stt=first_attn[bank],
                                   gsp=grp_stop, atps=atps: e.matmul(
                            atps[:, 1024 + a:1024 + b], vn[par][:, j * 128:(j + 1) * 128],
                            expt[my_epar][:, 0:b - a], start=stt, stop=gsp),
                            waits=([tbcs_done[(epoch + 1) % 2]]
                                   if first_attn[bank] else []),
                            inc="s_t")
                        first_attn[bank] = False
                        exp_free[my_epar] = tpv
                pend_z = {"e": epoch, "n": n, "qh": qh, "atps": atps,
                          "tlast": p.tokt("s_t")}
                epoch += 1
            if n + 2 < NH:
                hload[n + 2] = load_head(n + 2, [p.tokt("s_t")])
        emit_zchain(pend_z)
        pend_z = None

        p.phase([])
        BAR2 = barrier()
        if "ATT_t" in tap_outs:
            emit("sync", lambda e: e.dma_start(tap_outs["ATT_t"][:, :], ATTNT[:, :]),
                 waits=BAR2, inc="st_tap", dma=True)
        if nupto < 3:
            finish(p, emit)
            p.replay(block)
            return nc

        p.phase(BAR2)
        wotoks = [t_spw, None, None, None]
        wotoks[1] = emit("gpsimd", lambda e: e.dma_start(
            woq[1].ap().rearrange("p (a c) -> p a c", a=16),
            wor[:, :, 512:1024]), waits=BAR2, inc="w_wo1", dma=True)
        wotoks[2] = emit("gpsimd", lambda e: e.dma_start(
            woq[0].ap().rearrange("p (a c) -> p a c", a=16),
            wor[:, :, 1024:1536]), waits=BAR2, inc="w_wo2", dma=True)

        xt_free = [None, None, None]
        ev_prev = [None, None, None, None]
        sq_free = [None, None]
        gate_free = None
        gt8_free = None
        gsp_prev = None
        tt8_prev = [None, None]
        chain_done = [None, None]
        gsp_toks = []
        pending = None

        def gate_chain(i, par, tgl):
            nonlocal gt8_free, gsp_prev
            GLp = STATS[:, 48 + 8 * par: 56 + 8 * par]
            vchain(lambda e: e.reduce_max(STATS[:, 27:28], GLp, axis=AX.X),
                   extra=[tgl])
            vchain(lambda e, par=par: e.tensor_scalar(
                gwt[par][:, :], GLp, STATS[:, 27:28], None, ALU.is_equal),
                extra=[tt8_prev[par]])
            vchain(lambda e, par=par: e.scalar_tensor_tensor(
                STATS[:, 16:24], gwt[par][:, :], -1e30, GLp, ALU.mult, ALU.add))
            vchain(lambda e: e.reduce_max(STATS[:, 28:29], STATS[:, 16:24], axis=AX.X))
            t_d = vchain(lambda e: e.tensor_sub(STATS[:, 29:30], STATS[:, 28:29],
                                                STATS[:, 27:28]))
            te1 = emit("scalar", lambda e: e.activation(STATS[:, 40:41], STATS[:, 29:30],
                                                        AF.Exp, scale=-1.0),
                       waits=[t_d], inc="s_s")
            te2 = emit("scalar", lambda e: e.activation(STATS[:, 41:42], STATS[:, 29:30],
                                                        AF.Exp), waits=[te1], inc="s_s")
            vchain(lambda e: e.tensor_scalar_add(STATS[:, 42:44], STATS[:, 40:42], 1.0),
                   extra=[te2])
            vchain(lambda e: e.reciprocal(STATS[:, 44:46], STATS[:, 42:44]))
            vchain(lambda e, par=par: e.tensor_scalar(
                STATS[:, 32:40], GLp, STATS[:, 28:29], None, ALU.is_ge))
            vchain(lambda e, par=par: e.tensor_sub(STATS[:, 16:24], STATS[:, 32:40],
                                                   gwt[par][:, :]))
            vchain(lambda e: e.tensor_scalar(STATS[:, 16:24], STATS[:, 16:24],
                                             STATS[:, 44:45], None, ALU.mult))
            vchain(lambda e, par=par: e.tensor_scalar(
                gwt[par][:, :], gwt[par][:, :], STATS[:, 45:46], None, ALU.mult))
            t_w = vchain(lambda e, par=par: e.tensor_add(gwt[par][:, :], gwt[par][:, :],
                                                         STATS[:, 16:24]))
            chain_done[par] = t_w
            tt8 = emit("tensor", lambda e, par=par: e.matmul(
                pB[0:8, 512:640], gwt[par][:, :], IDF[:, :],
                start=True, stop=True, is_transpose=True),
                waits=[t_w, gt8_free], inc="s_t")
            tt8_prev[par] = tt8
            tv8 = emit("vector", lambda e: e.tensor_copy(gt8[:, :], pB[0:8, 512:640]),
                       waits=[tt8, gsp_prev], inc="s_v")
            gt8_free = tv8
            tsp = emit("sync", lambda e, i=i: e.dma_start(
                GSP[:, i * 128:(i + 1) * 128], gt8[:, :]),
                waits=[tv8], inc="st_gsp", dma=True)
            gsp_prev = tsp
            gsp_toks.append(tsp)

        def p3b_finish(pend):
            nonlocal ev_prev, gate_free, pending
            j, jpar, jbpar, jst = pend
            if pending is not None:
                gate_chain(*pending)
                pending = None
            tx = ln_rest(xt[jpar], sqb[jbpar], *jst, LNCOLS[j % 2])
            xt_free[jpar] = tx
            ev_prev, sq_free[jbpar] = emit_transposes(j, ev_prev, tx, sqb[jbpar])
            tev_all = ev_prev[3]
            for db in range(ND):
                emit("tensor", lambda e, db=db, j=j: e.matmul(
                    pB[:, 0:8],
                    XLNT[:, db * 2048 + j * 128: db * 2048 + (j + 1) * 128],
                    GWTB[:, db * 8:(db + 1) * 8],
                    start=(db == 0), stop=(db == ND - 1)),
                    waits=[tev_all, gate_free], inc="s_t")
            tg = p.tokt("s_t")
            tgl = emit("scalar", lambda e, jbpar=jbpar: e.copy(
                STATS[:, 48 + 8 * jbpar: 56 + 8 * jbpar], pB[:, 0:8]),
                waits=[tg, chain_done[jbpar]], inc="s_s")
            gate_free = tgl
            pending = (j, jbpar, tgl)

        psfree = {b: None for b in range(4)}
        xqfree = [None, None, None]
        x2free = [None, None, None]
        p3b_lts = {}
        p3b_st = {}

        def p3b_stats(j):
            p3b_st[j] = ln_stats(xt[j % 3], sqb[j % 2], p3b_lts[j],
                                 [sq_free[j % 2]], LNCOLS[j % 2])

        def p3b_fin(j):
            p3b_finish((j, j % 3, j % 2, p3b_st.pop(j)))

        for dq in range(4):
            wbuf = SPWOQ if dq == 0 else woq[dq % 2]
            p.phase(BAR2 + [wotoks[dq]])
            if dq == 3:
                tok_dq2 = p.tokt("s_t")
                ev_prev = [psfree[0], psfree[2], None, None]
            for i in range(NT):
                par = i % 3
                bank = 2 * (i % 2) + (dq % 2)
                pcol = bank * 512
                lt = emit("sync", lambda e, i=i, dq=dq, par=par: e.dma_start(
                    xq[par][:, :], xin[i * 128:(i + 1) * 128, dq * 512:(dq + 1) * 512]),
                    waits=[xqfree[par]], inc=f"l_xq{par}", dma=True)
                for nb in range(NH):
                    emit("tensor", lambda e, nb=nb, i=i, pcol=pcol, wbuf=wbuf: e.matmul(
                        pA[:, pcol:pcol + 512],
                        ATTNT[:, nb * 2048 + i * 128: nb * 2048 + (i + 1) * 128],
                        wbuf[:, nb * 512:(nb + 1) * 512],
                        start=(nb == 0), stop=(nb == NH - 1)),
                        waits=[psfree[bank]], inc="s_t")
                tmm = p.tokt("s_t")
                tadd = emit("vector", lambda e, par=par, pcol=pcol: e.tensor_add(
                    x2q[par][:, :], xq[par][:, :], pA[:, pcol:pcol + 512]),
                    waits=[tmm, lt, x2free[par]], inc="s_v")
                psfree[bank] = tadd
                xqfree[par] = tadd
                tst = emit("sync", lambda e, i=i, dq=dq, par=par: e.dma_start(
                    X2_sp[i * 128:(i + 1) * 128, dq * 512:(dq + 1) * 512], x2q[par][:, :]),
                    waits=[tadd], inc=f"st_x2{par}", dma=True)
                x2free[par] = tst
                if dq == 3:
                    if i >= 3:
                        p3b_fin(i - 3)
                    p3b_lts[i] = load_xt(
                        i, i % 3, X2_sp,
                        extra=[tst] + ([tok_dq2] if i < 2 else []))
                    if i >= 1:
                        p3b_stats(i - 1)
            if dq == 1:
                wotoks[3] = emit("gpsimd", lambda e: e.dma_start(
                    woq[1].ap().rearrange("p (a c) -> p a c", a=16),
                    wor[:, :, 1536:2048]),
                    waits=[p.tokt("s_t")], inc="w_wo1", dma=True)
        p3b_fin(NT - 3)
        p3b_stats(NT - 1)
        p3b_fin(NT - 2)
        p3b_fin(NT - 1)
        gate_chain(*pending)
        pending = None

        p.phase([])
        BAR3 = barrier(extra_st=True)
        w2r = w2.rearrange("E (a p) d -> E p a d", p=128)
        w2tok00 = emit("gpsimd", lambda e: e.dma_start(
            w2s[0].ap().rearrange("p (a c) -> p a c", a=NIB),
            w2r[0, :, :, 0:1024]), waits=BAR3, inc="w_w20", dma=True)
        if "X2_t" in tap_outs:
            emit("sync", lambda e: e.dma_start(tap_outs["X2_t"][:, :], X2_sp[:, :]),
                 waits=BAR3, inc="st_tap", dma=True)
        if nupto < 4:
            finish(p, emit)
            p.replay(block)
            return nc

        BAR3B = BAR3
        if "XLNT2_t" in tap_outs:
            emit("sync", lambda e: e.dma_start(tap_outs["XLNT2_t"][:, :], XLNT[:, :]),
                 waits=BAR3B, inc="st_tap", dma=True)
        if "GSP_t" in tap_outs:
            emit("sync", lambda e: e.dma_start(tap_outs["GSP_t"][:, :], GSP[:, :]),
                 waits=BAR3B, inc="st_tap", dma=True)
        if nupto < 5:
            finish(p, emit)
            p.replay(block)
            return nc

        p.phase(BAR3B)
        seq = [(pe, ib) for pe in range(NPE) for ib in range(NIB)]
        wtoks = {}
        wtoks[seq[0]] = w13_tok0
        wtoks[seq[1]] = w13_tok1

        h_free = None
        g_free = None
        silu_free = None
        afin_free = [None, None]
        grtok = [None, None]
        grb_free = [None, None]
        beb_use = [None, None]
        tbev_pe = {}

        def grb_load(pe):
            gpar = pe % 2
            grtok[gpar] = emit("gpsimd", lambda e, pe=pe, gpar=gpar: e.dma_start(
                grb[gpar][:, :], GSP[pe:pe + 1, :]),
                waits=[gsp_toks[-1], grb_free[gpar]], inc=f"w_gr{gpar}", dma=True)

        def emit_gate_bcast(pe, waits_extra=(), ps=None):
            gpar = pe % 2
            ps = pA if ps is None else ps
            for c in range(4):
                emit("tensor", lambda e, c=c, gpar=gpar, ps=ps: e.matmul(
                    ps[:, c * 512:(c + 1) * 512], ONESR[:, :],
                    grb[gpar][0:1, c * 512:(c + 1) * 512], start=True, stop=True),
                    waits=[grtok[gpar]] + list(waits_extra), inc="s_t")
            tbe = p.tokt("s_t")
            grb_free[gpar] = tbe
            tb = emit("scalar", lambda e, gpar=gpar, ps=ps: e.activation(
                beb[gpar][:, :], ps[:, :], AF.Copy),
                waits=[tbe, beb_use[gpar]], inc="s_s")
            tbev_pe[pe] = tb
            return tb

        p.phase(BAR3B)
        grb_load(0)
        emit_gate_bcast(0, ps=pB)
        g_free = tbev_pe[0]
        at0_done = None
        pre_asl = None
        pre_x2 = None
        for pe in range(NPE):
            p.phase(BAR3B)
            if pe == 9:
                pre_asl = emit("scalar", lambda e: e.dma_start(
                    aslg[0][:, :].rearrange("p (a c) -> p a c", a=NIB),
                    AT_sp[0, :, :, 0:512].transpose([1, 0, 2])),
                    waits=at0_done, inc="l_asl0", dma=True)
                pre_x2 = emit("sync", lambda e: e.dma_start(
                    x2b[0][:, :], X2_sp[0:128, 0:1024]),
                    waits=[beb_use[1]], inc="l_xq0", dma=True)
            for ib in range(NIB):
                gidx = pe * NIB + ib
                slot = gidx % 2
                for db in range(ND):
                    for c in range(4):
                        emit("tensor", lambda e, db=db, c=c, slot=slot: e.matmul(
                            pA[:, c * 512:(c + 1) * 512],
                            w1c[slot][:, db * 128:(db + 1) * 128],
                            XLNT[:, db * 2048 + c * 512: db * 2048 + (c + 1) * 512],
                            start=(db == 0), stop=(db == ND - 1)),
                            waits=[wtoks[(pe, ib)], h_free], inc="s_t")
                th = p.tokt("s_t")
                tsilu = emit("scalar", lambda e: e.activation(silu[:, :], pA[:, :], AF.Silu),
                             waits=[th, silu_free], inc="s_s")
                h_free = tsilu
                if ib == NIB - 2 and pe + 1 < 8:
                    grb_load(pe + 1)
                if ib == NIB - 1 and pe + 1 < 8:
                    emit_gate_bcast(pe + 1, waits_extra=[tsilu])
                    h_free = tbev_pe[pe + 1]
                for db in range(ND):
                    for c in range(4):
                        emit("tensor", lambda e, db=db, c=c, slot=slot: e.matmul(
                            pB[:, c * 512:(c + 1) * 512],
                            w3c[slot][:, db * 128:(db + 1) * 128],
                            XLNT[:, db * 2048 + c * 512: db * 2048 + (c + 1) * 512],
                            start=(db == 0), stop=(db == ND - 1)),
                            waits=[g_free], inc="s_t")
                tg2 = p.tokt("s_t")
                if gidx + 2 < len(seq):
                    nxt = seq[gidx + 2]
                    wtoks[nxt] = w13_load(nxt[0], nxt[1], (gidx + 2) % 2, [tg2])
                apar = gidx % 2
                if pe < 8:
                    gpar = pe % 2
                    ta = emit("vector", lambda e: e.tensor_mul(atmp[:, :], silu[:, :], pB[:, :]),
                              waits=[tg2, tsilu], inc="s_v")
                    tfin = emit("vector", lambda e, apar=apar, gpar=gpar: e.tensor_mul(
                        afin[apar][:, :], atmp[:, :], beb[gpar][:, :]),
                        waits=[ta, afin_free[apar], tbev_pe[pe]], inc="s_v")
                    beb_use[gpar] = tfin
                else:
                    tfin = emit("vector", lambda e, apar=apar: e.tensor_mul(
                        afin[apar][:, :], silu[:, :], pB[:, :]),
                        waits=[tg2, tsilu, afin_free[apar]], inc="s_v")
                g_free = tfin
                silu_free = tfin
                tsp = emit("sync", lambda e, pe=pe, ib=ib, apar=apar: e.dma_start(
                    AT_sp[pe, ib, :, :], afin[apar][:, :]),
                    waits=[tfin], inc=f"st_af{apar}", dma=True)
                afin_free[apar] = tsp
            if pe == 0:
                at0_done = [p.tokt("st_af0"), p.tokt("st_af1")]

        p.phase([])
        BAR5A = barrier(extra_st=True)
        if "AT_t" in tap_outs:
            emit("sync", lambda e: e.dma_start(tap_outs["AT_t"][:, :, :, :],
                                               AT_sp[:, :, :, :]),
                 waits=BAR5A, inc="st_tap", dma=True)
        if nupto < 6:
            finish(p, emit)
            p.replay(block)
            return nc

        sw2r = sw2.rearrange("(g a p) d -> g p a d", g=2, p=128)

        def w2_load(dh, pe, waits):
            if pe < 8:
                src = w2r[pe, :, :, dh * 1024:(dh + 1) * 1024]
            else:
                src = sw2r[pe - 8, :, :, dh * 1024:(dh + 1) * 1024]
            return emit("gpsimd", lambda e, src=src, pe=pe: e.dma_start(
                w2s[pe % 2].ap().rearrange("p (a c) -> p a c", a=NIB), src),
                waits=waits, inc=f"w_w2{pe % 2}", dma=True)

        yacc_done = [None] * NT
        w2s_free = [None, None]
        aslfree = [None, None]
        ps_free = {0: None, 1: None, 2: None, 3: None}
        x2b_free = [None, None]
        cur_lt = None
        cur_ap2 = 0
        for dh in range(2):
            p.phase(BAR5A)
            prev_out = p.tokt("st_out") if dh == 1 else None
            w2tok = {0: w2tok00 if dh == 0 else w2_load(dh, 0, [w2s_free[0]])}
            x2tok = [None, None]
            w2_consumed = {}
            for pe in range(NPE):
                p.phase(BAR5A + [w2tok[pe]])
                for k in range(NT):
                    kg, kk = k // 4, k % 4
                    apar = (pe * NT + k) % 4
                    psl = [(pA, 0), (pB, 0), (pA, 1024), (pB, 1024)][apar]
                    if pe == 0 and not (dh == 0 and k == 0):
                        x2tok[k % 2] = emit("sync", lambda e, k=k, dh=dh:
                                            e.dma_start(
                            x2b[k % 2][:, :],
                            X2_sp[k * 128:(k + 1) * 128, dh * 1024:(dh + 1) * 1024]),
                            waits=[x2b_free[k % 2]], inc=f"l_xq{k % 2}", dma=True)
                    elif pe == 0:
                        x2tok[0] = pre_x2
                    if kk == 0:
                        cur_ap2 = ((dh * NPE + pe) * 4 + kg) % 2
                        if dh == 0 and pe == 0 and kg == 0:
                            cur_lt = pre_asl
                        else:
                            cur_lt = emit("scalar", lambda e, pe=pe, kg=kg, dh=dh,
                                          ap2=cur_ap2: e.dma_start(
                                aslg[ap2][:, :].rearrange("p (a c) -> p a c", a=NIB),
                                AT_sp[pe, :, :, kg * 512:(kg + 1) * 512].transpose(
                                    [1, 0, 2])),
                                waits=[aslfree[cur_ap2]],
                                inc=f"l_asl{cur_ap2}", dma=True)
                    for ib in range(NIB):
                        for c in range(2):
                            emit("tensor", lambda e, ib=ib, c=c, kk=kk, pe=pe,
                                 ap2=cur_ap2, psl=psl: e.matmul(
                                psl[0][:, psl[1] + c * 512: psl[1] + (c + 1) * 512],
                                aslg[ap2][:, ib * 512 + kk * 128:
                                          ib * 512 + (kk + 1) * 128],
                                w2s[pe % 2][:, ib * 1024 + c * 512:
                                            ib * 1024 + (c + 1) * 512],
                                start=(ib == 0), stop=(ib == NIB - 1)),
                                waits=[cur_lt, ps_free[apar]], inc="s_t")
                    tmm = p.tokt("s_t")
                    if kk == 3:
                        aslfree[cur_ap2] = tmm
                    if pe == 0:
                        tac = emit("vector", lambda e, k=k, psl=psl, kpar=k % 2: e.tensor_add(
                            Yacc[k][:, :], x2b[kpar][:, :],
                            psl[0][:, psl[1]:psl[1] + 1024]),
                            waits=[tmm, x2tok[k % 2], prev_out, yacc_done[k]],
                            inc="s_v")
                        x2b_free[k % 2] = tac
                    else:
                        tac = emit("vector", lambda e, k=k, psl=psl: e.tensor_add(
                            Yacc[k][:, :], Yacc[k][:, :],
                            psl[0][:, psl[1]:psl[1] + 1024]),
                            waits=[tmm, yacc_done[k]], inc="s_v")
                    ps_free[apar] = tac
                    yacc_done[k] = tac
                    if pe == NPE - 1:
                        emit("sync", lambda e, k=k, dh=dh: e.dma_start(
                            out[k * 128:(k + 1) * 128, dh * 1024:(dh + 1) * 1024],
                            Yacc[k][:, :]),
                            waits=[tac], inc="st_out", dma=True)
                w2_consumed[pe] = p.tokt("s_t")
                if pe + 1 < NPE:
                    w2tok[pe + 1] = w2_load(
                        dh, pe + 1,
                        [w2_consumed[pe - 1]] if pe >= 1 else [w2s_free[1]])
            w2s_free = [w2_consumed[8], w2_consumed[NPE - 1]]

        finish(p, emit)
        p.replay(block)
    return nc


def finish(p, emit):
    final = [(s, p.cnt[s]) for s in SEM_NAMES if p.cnt[s] > 0]
    p.phase([])
    emit("sync", lambda e: None, waits=final)


def _make_consts():
    ident = np.eye(128, dtype=np.float32)
    triu = np.triu(np.ones((128, 128), dtype=np.float32))
    perm = np.zeros((128, 128), dtype=np.float32)
    for h in range(128):
        perm[h, h ^ 1] = 1.0
    i = np.arange(64, dtype=np.float64)
    ts = 10000.0 ** (i / 64.0)
    t = np.arange(S, dtype=np.float64)
    ang = t[None, :] / ts[:, None]
    cos = np.cos(ang)
    sin = np.sin(ang)
    cosf = np.repeat(cos, 2, axis=0).astype(np.float32)
    ssin = np.empty((128, S), dtype=np.float32)
    ssin[0::2, :] = -sin
    ssin[1::2, :] = sin
    return ident, triu, perm, cosf, ssin


def _in_maps(kw, x):
    ident, triu, perm, cosf, ssin = _make_consts()
    base = {
        "wq": np.ascontiguousarray(np.asarray(kw["L0_attn_wq"], dtype=np.float32).reshape(D, D)),
        "wk": np.ascontiguousarray(np.asarray(kw["L0_attn_wk"], dtype=np.float32).reshape(D, D)),
        "wv": np.ascontiguousarray(np.asarray(kw["L0_attn_wv"], dtype=np.float32).reshape(D, D)),
        "wo": np.ascontiguousarray(np.asarray(kw["L0_attn_wo"], dtype=np.float32).reshape(D, D)),
        "gwT": np.ascontiguousarray(np.asarray(kw["L0_ffn_gate_w"], dtype=np.float32).T),
        "w1": np.ascontiguousarray(np.asarray(kw["L0_ffn_w1"], dtype=np.float32)),
        "w3": np.ascontiguousarray(np.asarray(kw["L0_ffn_w3"], dtype=np.float32)),
        "w2": np.ascontiguousarray(np.asarray(kw["L0_ffn_w2"], dtype=np.float32)),
        "sw1": np.ascontiguousarray(np.asarray(kw["L0_ffn_sw1"], dtype=np.float32)),
        "sw3": np.ascontiguousarray(np.asarray(kw["L0_ffn_sw3"], dtype=np.float32)),
        "sw2": np.ascontiguousarray(np.asarray(kw["L0_ffn_sw2"], dtype=np.float32)),
        "cident": ident, "ctriu": triu, "cperm": perm,
        "ccos": cosf, "cssin": ssin,
    }
    return [dict(base, xin=np.ascontiguousarray(x[c])) for c in range(8)]


def kernel(L0_attn_x, L0_attn_wq, L0_attn_wk, L0_attn_wv, L0_attn_wo,
           L0_ffn_gate_w, L0_ffn_w1, L0_ffn_w3, L0_ffn_w2,
           L0_ffn_sw1, L0_ffn_sw3, L0_ffn_sw2):
    kw = dict(L0_attn_wq=L0_attn_wq, L0_attn_wk=L0_attn_wk, L0_attn_wv=L0_attn_wv,
              L0_attn_wo=L0_attn_wo, L0_ffn_gate_w=L0_ffn_gate_w, L0_ffn_w1=L0_ffn_w1,
              L0_ffn_w3=L0_ffn_w3, L0_ffn_w2=L0_ffn_w2, L0_ffn_sw1=L0_ffn_sw1,
              L0_ffn_sw3=L0_ffn_sw3, L0_ffn_sw2=L0_ffn_sw2)
    x = np.asarray(L0_attn_x, dtype=np.float32)
    nc = build()
    res = run_bass_kernel_spmd(nc, _in_maps(kw, x), core_ids=list(range(8)))
    return np.stack([res.results[c]["out"] for c in range(8)], axis=0)


# version 28
# speedup vs baseline: 1.0370x; 1.0253x over previous
import contextlib
import math
import numpy as np

import concourse.bass as bass
import concourse.mybir as mybir
from concourse.bass_utils import run_bass_kernel_spmd

F32 = mybir.dt.float32
BF16 = mybir.dt.bfloat16
AF = mybir.ActivationFunctionType
ALU = mybir.AluOpType
AX = mybir.AxisListType

S = 2048
D = 2048
NT = 16
ND = 16
NH = 16
EI = 1408
NIB = 11
NPE = 10
QSCALE = 1.0 / math.sqrt(128.0)
EXPBIAS = -12.0
EPS = 1e-5

ENGINES = ("tensor", "vector", "scalar", "gpsimd", "sync")

RESV = 16512
SMALL = RESV
WB = SMALL + 2048
XLNT_OFF = WB + 45056
R1_OFF = XLNT_OFF + 65536
SPARE = R1_OFF + 65536

PHASES = ("p0", "p1", "p2", "p3", "p3b", "p5a", "p5b")

SEM_NAMES = (
    ["s_t", "s_v", "s_s", "s_g"]
    + ["w_c", "w_cs", "w_k", "w_v", "w_qc0", "w_qc1", "w_wo0", "w_wo1",
       "w_e0", "w_e1", "w_gr0", "w_gr1", "w_w20", "w_w21"]
    + ["l_xt0", "l_xt1", "l_xt2", "l_xs0", "l_xs1", "l_xs2",
       "l_xq0", "l_xq1", "l_xq2", "w_wo2", "l_h0", "l_h1",
       "l_asl0", "l_asl1", "l_asl2", "l_asl3", "l_yin"]
    + ["st_kq0", "st_kq1", "st_v0", "st_v1", "st_x20", "st_x21", "st_x22",
       "st_gsp", "st_af0", "st_af1", "st_out", "st_tap"]
)
DMA_SEMS = [s for s in SEM_NAMES
            if (s[0] in "wl" and s != "s_g") or s.startswith("st")]


class Prog:
    def __init__(self, nc):
        self.nc = nc
        self.ops = {e: [] for e in ENGINES}
        self.semh = {}
        self.cnt = {}
        self.phase_waits = []

    def add_sem(self, name, handle):
        self.semh[name] = handle
        self.cnt[name] = 0

    def phase(self, waits):
        self.phase_waits = [w for w in waits if w is not None]

    def emit(self, eng, fn, waits=(), inc=None, dma=False):
        amt = 16 if dma else 1
        tok = None
        incs = []
        if inc is not None:
            self.cnt[inc] += amt
            tok = (inc, self.cnt[inc])
            incs.append((inc, amt))
        w = self.phase_waits + [x for x in waits if x is not None]
        self.ops[eng].append([fn, w, incs])
        return tok

    def mark(self, eng, sem):
        assert self.ops[eng], f"no ops on {eng} to mark"
        rec = self.ops[eng][-1]
        for (s, _a) in rec[2]:
            if s == sem:
                return (sem, self.cnt[sem])
        self.cnt[sem] += 1
        rec[2].append((sem, 1))
        return (sem, self.cnt[sem])

    def tokt(self, sem):
        return (sem, self.cnt[sem])

    def all_dma_toks(self):
        return [(s, self.cnt[s]) for s in DMA_SEMS if self.cnt[s] > 0]

    def replay(self, block):
        for eng_name in ENGINES:
            section = getattr(block, eng_name)
            oplist = self.ops[eng_name]
            semh = self.semh

            def make(oplist=oplist):
                lastwait = {}

                def _(eng):
                    for fn, waits, incs in oplist:
                        for (s, v) in waits:
                            if v > 0 and lastwait.get(s, -1) < v:
                                eng.wait_ge(semh[s], v)
                                lastwait[s] = v
                        inst = fn(eng)
                        if inst is None:
                            assert not incs
                            continue
                        for (s, amt) in incs:
                            r = inst.then_inc(semh[s], amt)
                            if r is not None:
                                inst = r
                return _

            section(make())


def build(taps=(), upto="p5b"):
    nupto = PHASES.index(upto)
    nc = bass.Bass("TRN2", target_bir_lowering=False)
    assert nc.sbuf_base <= SMALL, f"bass internal allocs reached {nc.sbuf_base}"

    xin = nc.dram_tensor("xin", [S, D], F32, kind="ExternalInput")
    wq = nc.dram_tensor("wq", [D, D], F32, kind="ExternalInput")
    wk = nc.dram_tensor("wk", [D, D], F32, kind="ExternalInput")
    wv = nc.dram_tensor("wv", [D, D], F32, kind="ExternalInput")
    wo = nc.dram_tensor("wo", [D, D], F32, kind="ExternalInput")
    gwT = nc.dram_tensor("gwT", [D, 8], F32, kind="ExternalInput")
    w1 = nc.dram_tensor("w1", [8, D, EI], F32, kind="ExternalInput")
    w3 = nc.dram_tensor("w3", [8, D, EI], F32, kind="ExternalInput")
    w2 = nc.dram_tensor("w2", [8, EI, D], F32, kind="ExternalInput")
    sw1 = nc.dram_tensor("sw1", [D, 2 * EI], F32, kind="ExternalInput")
    sw3 = nc.dram_tensor("sw3", [D, 2 * EI], F32, kind="ExternalInput")
    sw2 = nc.dram_tensor("sw2", [2 * EI, D], F32, kind="ExternalInput")
    cident = nc.dram_tensor("cident", [128, 128], F32, kind="ExternalInput")
    ctriu = nc.dram_tensor("ctriu", [128, 128], F32, kind="ExternalInput")
    cperm = nc.dram_tensor("cperm", [128, 128], F32, kind="ExternalInput")
    ccos = nc.dram_tensor("ccos", [128, S], F32, kind="ExternalInput")
    cssin = nc.dram_tensor("cssin", [128, S], F32, kind="ExternalInput")
    out = nc.dram_tensor("out", [S, D], F32, kind="ExternalOutput")

    QT_sp = nc.dram_tensor("QT_sp", [D, S], BF16, kind="Internal")
    KT_sp = nc.dram_tensor("KT_sp", [D, S], BF16, kind="Internal")
    V_sp = nc.dram_tensor("V_sp", [S, D], BF16, kind="Internal")
    X2_sp = nc.dram_tensor("X2_sp", [S, D], F32, kind="Internal")
    AT_sp = nc.dram_tensor("AT_sp", [NPE, NIB, 128, S], BF16, kind="Internal")
    GSP = nc.dram_tensor("GSP", [8, S], F32, kind="Internal")

    tap_outs = {}
    tap_specs = {"XLNT_t": ([128, 16 * 2048], BF16), "QT_t": ([D, S], BF16),
                 "KT_t": ([D, S], BF16), "V_t": ([S, D], BF16),
                 "ATT_t": ([128, 16 * 2048], BF16), "X2_t": ([S, D], F32),
                 "GSP_t": ([8, S], F32), "STATS_t": ([128, 48], F32),
                 "XLNT2_t": ([128, 16 * 2048], BF16),
                 "AT_t": ([NPE, NIB, 128, S], BF16)}
    for t in taps:
        shp, dt = tap_specs[t]
        tap_outs[t] = nc.dram_tensor(t, shp, dt, kind="ExternalOutput")

    A = nc.alloc_sbuf_tensor_at
    IDF = A("IDF", [128, 128], F32, offset=SMALL + 0)
    TRIUB = A("TRIUB", [128, 128], BF16, offset=SMALL + 512)
    PERMB = A("PERMB", [128, 128], BF16, offset=SMALL + 768)
    ONESC = A("ONESC", [128, 1], BF16, offset=SMALL + 1024)
    ONESR = A("ONESR", [1, 128], BF16, offset=SMALL + 1056)
    GWTB = A("GWTB", [128, 128], BF16, offset=SMALL + 1312)
    STATS = A("STATS", [128, 64], F32, offset=SMALL + 1568)
    EXB = A("EXB", [128, 1], F32, offset=SMALL + 1824)

    xt = [A("xt0", [128, D], F32, offset=WB + 0),
          A("xt1", [128, D], F32, offset=WB + 8192),
          A("xt2", [128, D], F32, offset=SPARE + 8192)]
    sqb = [A("sqb0", [128, D], BF16, offset=SPARE + 0),
           A("sqb1", [128, D], BF16, offset=SPARE + 4096)]
    gwt = [A("gwt0", [128, 8], F32, offset=SPARE + 33024),
           A("gwt1", [128, 8], F32, offset=SPARE + 33056)]
    gt8 = A("gt8", [8, 128], F32, offset=SPARE + 33088)
    IDB = A("IDB", [128, 128], BF16, offset=SPARE + 32768)
    SPWOQ = A("SPWOQ", [128, 16 * 512], BF16, offset=SPARE + 0)

    kr = [A("kr0", [128, S], BF16, offset=WB + 0),
          A("kr1", [128, S], BF16, offset=WB + 4096)]
    t1b = A("t1b", [128, S], BF16, offset=WB + 8192)
    t2b = A("t2b", [128, S], BF16, offset=WB + 12288)
    krp = [A("krp0", [128, S], BF16, offset=WB + 16384),
           A("krp1", [128, S], BF16, offset=WB + 20480)]
    qc = [A("qc0", [128, S], BF16, offset=WB + 24576),
          A("qc1", [128, S], BF16, offset=WB + 28672)]
    COSB = A("COSB", [128, S], BF16, offset=WB + 32768)
    SSINB = A("SSINB", [128, S], BF16, offset=WB + 36864)

    qtn = [A("qtn0", [128, S], BF16, offset=WB + 0),
           A("qtn1", [128, S], BF16, offset=WB + 4096)]
    ktn = [A("ktn0", [128, S], BF16, offset=WB + 8192),
           A("ktn1", [128, S], BF16, offset=WB + 12288)]
    vn = [A("vn0", [128, S], BF16, offset=WB + 16384),
          A("vn1", [128, S], BF16, offset=WB + 20480)]
    expt = [A("exp0", [128, 512], BF16, offset=WB + 24576),
            A("exp1", [128, 512], BF16, offset=WB + 25600),
            A("exp2", [128, 512], BF16, offset=WB + 41984),
            A("exp3", [128, 512], BF16, offset=WB + 43008)]
    bcs = A("bcs", [128, 1024], BF16, offset=WB + 26624)
    zsb = A("zsb", [1, 1024], F32, offset=WB + 28672)
    recf = A("recf", [1, 1024], F32, offset=WB + 32768)
    recb = A("recb", [1, 1024], BF16, offset=WB + 36864)

    woq = [A("woq0", [128, 16 * 512], BF16, offset=WB + 0),
           A("woq1", [128, 16 * 512], BF16, offset=WB + 16384)]
    xq = [A("xq0", [128, 512], F32, offset=WB + 32768),
          A("xq1", [128, 512], F32, offset=WB + 34816),
          A("xq2", [128, 512], F32, offset=WB + 36864)]
    x2q = [A("x2q0", [128, 512], F32, offset=WB + 38912),
           A("x2q1", [128, 512], F32, offset=WB + 40960),
           A("x2q2", [128, 512], F32, offset=WB + 43008)]

    w1c = [A("w1c0", [128, S], BF16, offset=SPARE + 16384),
           A("w1c1", [128, S], BF16, offset=SPARE + 24576)]
    w3c = [A("w3c0", [128, S], BF16, offset=SPARE + 20480),
           A("w3c1", [128, S], BF16, offset=SPARE + 28672)]
    silu = A("silu", [128, S], BF16, offset=WB + 16384)
    atmp = A("atmp", [128, S], BF16, offset=WB + 20480)
    afin = [A("afin0", [128, S], BF16, offset=WB + 24576),
            A("afin1", [128, S], BF16, offset=WB + 28672)]
    beb = [A("beb0", [128, S], BF16, offset=WB + 32768),
           A("beb1", [128, S], BF16, offset=WB + 0)]
    grb = [A("gr0", [1, S], BF16, offset=WB + 36864),
           A("gr1", [1, S], BF16, offset=WB + 40960)]

    aslg = [A("aslg0", [128, NIB * 512], BF16, offset=WB + 4096),
            A("aslg1", [128, NIB * 512], BF16, offset=WB + 15360)]
    x2b = [A("x2b0", [128, 1024], F32, offset=WB + 0),
           A("x2b1", [128, 1024], F32, offset=WB + 26624)]

    XLNT = A("XLNT", [128, 16 * 2048], BF16, offset=XLNT_OFF)
    Yacc = [A(f"Yacc{k}", [128, 1024], F32, offset=XLNT_OFF + 4096 * k)
            for k in range(NT)]
    WKV = A("WKV", [128, 16 * 2048], BF16, offset=R1_OFF)
    ATTNT = A("ATTNT", [128, 16 * 2048], BF16, offset=R1_OFF)
    w2s = [A("w2s0", [128, NIB * 1024], BF16, offset=R1_OFF + 0),
           A("w2s1", [128, NIB * 1024], BF16, offset=R1_OFF + 22528)]

    pA = nc.alloc_psum_tensor("pA", [128, 2048], F32)
    pB = nc.alloc_psum_tensor("pB", [128, 2048], F32)

    with contextlib.ExitStack() as stack:
        p = Prog(nc)
        for name in SEM_NAMES:
            p.add_sem(name, stack.enter_context(nc.semaphore(name)))
        block = stack.enter_context(nc.Block())
        emit = p.emit
        mark = p.mark

        def barrier(extra_st=False):
            w = [mark("tensor", "s_t"), mark("vector", "s_v"), mark("scalar", "s_s")]
            if extra_st:
                w += p.all_dma_toks()
            return w

        p.phase([])
        t_consts = None
        for (dstt, srct) in ((IDF, cident), (TRIUB, ctriu), (PERMB, cperm)):
            t_consts = emit("gpsimd",
                            lambda e, d=dstt, s=srct: e.dma_start(d[:, :], s[:, :]),
                            inc="w_c", dma=True)
        gsrc = gwT.rearrange("(a p) e -> p a e", p=128)
        t_consts = emit("gpsimd",
                        lambda e: e.dma_start(
                            GWTB.ap().rearrange("p (a c) -> p a c", a=16)[:, :, 0:8],
                            gsrc),
                        inc="w_c", dma=True)
        t_wk = emit("gpsimd",
                    lambda e: e.dma_start(
                        WKV.ap().rearrange("p (a n) -> p a n", a=16),
                        wk.rearrange("(a p) n -> p a n", p=128)),
                    inc="w_k", dma=True)
        emit("vector", lambda e: e.memset(ONESC[:, :], 1.0))
        emit("vector", lambda e: e.memset(ONESR[:, :], 1.0))
        emit("vector", lambda e: e.memset(EXB[:, :], EXPBIAS))
        t_ones = mark("vector", "s_v")
        t_idb = emit("scalar", lambda e: e.activation(IDB[:, :], IDF[:, :], AF.Copy),
                     waits=[p.tokt("w_c")], inc="s_s")

        def vchain(fn, extra=()):
            return emit("vector", fn, waits=[p.tokt("s_v")] + list(extra), inc="s_v")

        LNCOLS = [[0, 1, 2, 3, 4, 6, 7, 8], [9, 10, 11, 12, 13, 14, 15, 24]]

        def c1(col):
            return STATS[:, col:col + 1]

        def ln_stats(xtile, dstb, src_toks, sq_waits, C):
            tsq = emit("scalar", lambda e: e.activation(dstb[:, :], xtile[:, :], AF.Square,
                                                        accum_out=c1(C[1])),
                       waits=list(src_toks) + list(sq_waits), inc="s_s")
            tsum = vchain(lambda e: e.reduce_sum(c1(C[0]), xtile[:, :], axis=AX.X),
                          extra=list(src_toks))
            return tsq, tsum

        def ln_rest(xtile, dstb, tsq, tsum, C):
            vchain(lambda e: e.tensor_scalar_mul(c1(C[2]), c1(C[0]), 1.0 / D),
                   extra=[tsum])
            vchain(lambda e: e.tensor_scalar(c1(C[3]), c1(C[1]), 1.0 / D, EPS,
                                             ALU.mult, ALU.add), extra=[tsq])
            vchain(lambda e: e.tensor_mul(c1(C[4]), c1(C[2]), c1(C[2])))
            tv = vchain(lambda e: e.tensor_sub(c1(C[5]), c1(C[3]), c1(C[4])))
            tln = emit("scalar", lambda e: e.activation(c1(C[6]), c1(C[5]), AF.Ln),
                       waits=[tv], inc="s_s")
            tr = emit("scalar", lambda e: e.activation(c1(C[7]), c1(C[6]), AF.Exp,
                                                       scale=-0.5), waits=[tln], inc="s_s")
            tnb = vchain(lambda e: e.scalar_tensor_tensor(c1(C[4]), c1(C[2]), -1.0,
                                                          c1(C[7]), ALU.mult, ALU.mult),
                         extra=[tr])
            emit("scalar", lambda e: e.activation(dstb[:, :], xtile[:, :], AF.Identity,
                                                  bias=c1(C[4]), scale=c1(C[7])),
                 waits=[tnb], inc="s_s")
            return p.tokt("s_s")

        def load_xt(i, par, srcten, extra=()):
            t1 = emit("sync", lambda e, i=i, par=par, srcten=srcten: e.dma_start(
                xt[par][:, :], srcten[i * 128:(i + 1) * 128, :]),
                waits=[xt_free[par]] + list(extra), inc=f"l_xt{par}", dma=True)
            return [t1]

        pAb = pA.ap().bitcast(BF16)
        pBb = pB.ap().bitcast(BF16)

        def emit_transposes(i, ev_prev, sink, sq):
            ev = [None, None, None, None]
            gco = (0, 2048, 2048, 3072)
            for g in range(4):
                psb = pAb if g < 2 else pBb
                coff = gco[g]
                for jj in range(4):
                    j = g * 4 + jj
                    emit("tensor", lambda e, psb=psb, coff=coff, jj=jj, j=j, sq=sq: e.matmul(
                        psb[:, coff + jj * 128: coff + (jj + 1) * 128],
                        sq[:, j * 128:(j + 1) * 128], IDB[:, :],
                        start=True, stop=True, is_transpose=True),
                        waits=[sink, ev_prev[g]], inc="s_t")
                tt = p.tokt("s_t")
                jbase = g * 4

                def evac(e, psb=psb, coff=coff, jbase=jbase, i=i):
                    dst = XLNT.ap().rearrange("p (a c) -> p a c", a=16)[
                        :, jbase:jbase + 4, i * 128:(i + 1) * 128]
                    return e.copy(dst, psb[:, coff:coff + 512].rearrange(
                        "p (a c) -> p a c", a=4))
                ev[g] = emit("scalar", evac, waits=[tt], inc="s_s")
            return ev, p.tokt("s_t")

        p.phase([p.tokt("w_c"), t_ones, t_idb])
        xt_free = [None, None, None]
        ev_prev = [None, None, None, None]
        sq_free = [None, None]
        pend_ln = None

        def ln_finish(pend):
            nonlocal ev_prev
            j, jpar, jbpar, jst = pend
            tx = ln_rest(xt[jpar], sqb[jbpar], *jst, LNCOLS[j % 2])
            xt_free[jpar] = tx
            ev_prev, sq_free[jbpar] = emit_transposes(j, ev_prev, tx, sqb[jbpar])

        lts_q = {0: load_xt(0, 0, xin), 1: load_xt(1, 1, xin)}
        for i in range(NT):
            par = i % 3
            bpar = i % 2
            if pend_ln is not None:
                ln_finish(pend_ln)
            if i + 2 < NT:
                lts_q[i + 2] = load_xt(i + 2, (i + 2) % 3, xin)
            st = ln_stats(xt[par], sqb[bpar], lts_q[i], [sq_free[bpar]], LNCOLS[i % 2])
            pend_ln = (i, par, bpar, st)
        ln_finish(pend_ln)
        pend_ln = None
        BAR0 = barrier()
        emit("gpsimd", lambda e: e.dma_start(COSB[:, :], ccos[:, :]),
             waits=BAR0, inc="w_cs", dma=True)
        cos_tok = emit("gpsimd", lambda e: e.dma_start(SSINB[:, :], cssin[:, :]),
                       inc="w_cs", dma=True)
        if "STATS_t" in tap_outs:
            emit("sync", lambda e: e.dma_start(tap_outs["STATS_t"][:, :], STATS[:, :]),
                 waits=BAR0, inc="st_tap", dma=True)
        if "XLNT_t" in tap_outs:
            emit("sync", lambda e: e.dma_start(tap_outs["XLNT_t"][:, :], XLNT[:, :]),
                 waits=BAR0, inc="st_tap", dma=True)
        if nupto < 1:
            finish(p, emit)
            p.replay(block)
            return nc

        wqr = wq.rearrange("(a p) n -> p a n", p=128)
        qctoks = [None] * NH
        TE = {}; TV2 = {}; TROPE = {}; TSP = {}; PSH = {}

        def proj_part(h, lhs_tile, lhs_kind, n, scale, wtok, mid=None):
            par = h % 2
            ps = pA if par == 0 else pB
            for db in range(ND):
                if db == 8 and mid is not None:
                    mid()
                lcol = db * 2048 + n * 128 if lhs_kind == "wkv" else db * 128
                for c in range(4):
                    emit("tensor", lambda e, db=db, c=c, lcol=lcol, ps=ps,
                         lhs_tile=lhs_tile: e.matmul(
                        ps[:, c * 512:(c + 1) * 512],
                        lhs_tile[:, lcol:lcol + 128],
                        XLNT[:, db * 2048 + c * 512: db * 2048 + (c + 1) * 512],
                        start=(db == 0), stop=(db == ND - 1)),
                        waits=[wtok], inc="s_t")
            tproj = p.tokt("s_t")
            TE[h] = emit("scalar", lambda e, par=par, scale=scale, ps=ps: e.activation(
                kr[par][:, :], ps[:, :], AF.Copy, scale=scale),
                waits=[tproj], inc="s_s")
            PSH[h] = ps

        def perm_part(h, n, spill):
            par = h % 2
            ps = PSH[h]
            for c in range(4):
                emit("tensor", lambda e, c=c, par=par, ps=ps: e.matmul(
                    ps[:, c * 512:(c + 1) * 512], PERMB[:, :],
                    kr[par][:, c * 512:(c + 1) * 512], start=True, stop=True),
                    waits=[TE[h]], inc="s_t")
            tperm = p.tokt("s_t")
            t1t = emit("vector", lambda e, par=par: e.tensor_mul(t1b[:, :], kr[par][:, :],
                                                                 COSB[:, :]),
                       waits=[TE[h], TROPE.get(h - 1)], inc="s_v")
            tv2 = emit("vector", lambda e, ps=ps: e.tensor_mul(t2b[:, :], ps[:, :],
                                                               SSINB[:, :]),
                       waits=[tperm], inc="s_v")
            trope = emit("vector", lambda e, par=par: e.tensor_add(krp[par][:, :], t1b[:, :],
                                                                   t2b[:, :]),
                         waits=[t1t, tv2, TSP.get(h - 2)], inc="s_v")
            tsp = emit("sync", lambda e, n=n, par=par, spill=spill: e.dma_start(
                spill[n * 128:(n + 1) * 128, :], krp[par][:, :]),
                waits=[trope], inc=f"st_kq{par}", dma=True)
            TV2[h] = tv2
            TROPE[h] = trope
            TSP[h] = tsp

        t_wv = None
        for h in range(2 * NH):
            n = h % NH
            p.phase(BAR0 + [cos_tok] + [t for t in [TV2.get(h - 2)] if t])
            if h == 0:
                for m in range(2):
                    qctoks[m] = emit("gpsimd", lambda e, m=m: e.dma_start(
                        qc[m % 2].ap().rearrange("p (a c) -> p a c", a=16),
                        wqr[:, :, m * 128:(m + 1) * 128]), inc=f"w_qc{m % 2}",
                        dma=True)
            mid = None
            if h >= 1:
                hp = h - 1
                mid = (lambda hp=hp: perm_part(hp, hp % NH,
                                               KT_sp if hp < NH else QT_sp))
            if h < NH:
                proj_part(h, WKV, "wkv", n, 1.0, p.tokt("w_k"), mid=mid)
            else:
                proj_part(h, qc[n % 2], "qcol", n, QSCALE, qctoks[n], mid=mid)
            tfree = p.tokt("s_t")
            if h == NH - 1:
                t_wv = emit("gpsimd", lambda e: e.dma_start(
                    WKV.ap().rearrange("p (a n) -> p a n", a=16),
                    wv.rearrange("(a p) n -> p a n", p=128)),
                    waits=[tfree], inc="w_v", dma=True)
            if h >= NH and n + 2 < NH:
                qctoks[n + 2] = emit("gpsimd", lambda e, n=n: e.dma_start(
                    qc[n % 2].ap().rearrange("p (a c) -> p a c", a=16),
                    wqr[:, :, (n + 2) * 128:(n + 3) * 128]),
                    waits=[tfree], inc=f"w_qc{n % 2}", dma=True)
        perm_part(2 * NH - 1, NH - 1, QT_sp)
        TQV = p.tokt("s_v")
        TQS = p.tokt("s_s")

        vevac = [None, None]
        vspill = [None, None]
        for i in range(NT):
            par = i % 2
            ps = pA if par == 0 else pB
            waits = [p.tokt("w_v"), TQV, TQS]
            if i >= 2:
                waits += [vevac[par], vspill[par]]
            p.phase(BAR0 + waits)
            for db in range(ND):
                for c in range(4):
                    emit("tensor", lambda e, db=db, c=c, ps=ps, i=i: e.matmul(
                        ps[:, c * 512:(c + 1) * 512],
                        XLNT[:, db * 2048 + i * 128: db * 2048 + (i + 1) * 128],
                        WKV[:, db * 2048 + c * 512: db * 2048 + (c + 1) * 512],
                        start=(db == 0), stop=(db == ND - 1)), inc="s_t")
            tmm = p.tokt("s_t")
            te = emit("scalar", lambda e, par=par, ps=ps: e.activation(
                kr[par][:, :], ps[:, :], AF.Copy),
                waits=[tmm], inc="s_s")
            vevac[par] = te
            tsp = emit("sync", lambda e, i=i, par=par: e.dma_start(
                V_sp[i * 128:(i + 1) * 128, :], kr[par][:, :]),
                waits=[te], inc=f"st_v{par}", dma=True)
            vspill[par] = tsp

        p.phase([])
        BAR1 = barrier(extra_st=True)
        for tname, src in (("QT_t", QT_sp), ("KT_t", KT_sp), ("V_t", V_sp)):
            if tname in tap_outs:
                emit("sync", lambda e, tname=tname, src=src: e.dma_start(
                    tap_outs[tname][:, :], src[:, :]), waits=BAR1, inc="st_tap", dma=True)
        if nupto < 2:
            finish(p, emit)
            p.replay(block)
            return nc

        hload = [None] * NH

        def load_head(n, waits):
            par = n % 2
            emit("sync", lambda e, n=n, par=par: e.dma_start(
                qtn[par][:, :], QT_sp[n * 128:(n + 1) * 128, :]),
                waits=waits, inc=f"l_h{par}", dma=True)
            emit("sync", lambda e, n=n, par=par: e.dma_start(
                ktn[par][:, :], KT_sp[n * 128:(n + 1) * 128, :]),
                inc=f"l_h{par}", dma=True)
            return emit("sync", lambda e, n=n, par=par: e.dma_start(
                vn[par][:, :].rearrange("p (a c) -> p a c", a=16),
                V_sp.rearrange("(a p) n -> p a n", p=128)[:, :, n * 128:(n + 1) * 128]),
                inc=f"l_h{par}", dma=True)

        p.phase(BAR1)
        hload[0] = load_head(0, [])
        hload[1] = load_head(1, [])
        wor = wo.rearrange("(a p) n -> p a n", p=128)
        t_spw = emit("gpsimd", lambda e: e.dma_start(
            SPWOQ.ap().rearrange("p (a c) -> p a c", a=16), wor[:, :, 0:512]),
            inc="w_wo0", dma=True)
        def wsrc(w_full, sw_full, pe):
            if pe < 8:
                return w_full.rearrange("E (a p) i -> E p a i", p=128)[pe, :, :, :]
            return sw_full.rearrange("(a p) i -> p a i", p=128)[
                :, :, (pe - 8) * EI:(pe - 7) * EI]

        def w13_load(pe, ib, slot, waits):
            emit("gpsimd", lambda e, pe=pe, ib=ib, slot=slot: e.dma_start(
                w1c[slot].ap().rearrange("p (a c) -> p a c", a=16),
                wsrc(w1, sw1, pe)[:, :, ib * 128:(ib + 1) * 128]),
                waits=waits, inc=f"w_e{slot}", dma=True)
            return emit("gpsimd", lambda e, pe=pe, ib=ib, slot=slot: e.dma_start(
                w3c[slot].ap().rearrange("p (a c) -> p a c", a=16),
                wsrc(w3, sw3, pe)[:, :, ib * 128:(ib + 1) * 128]),
                inc=f"w_e{slot}", dma=True)

        w13_tok0 = w13_load(0, 0, 0, [])
        w13_tok1 = w13_load(0, 1, 1, [])

        piece_free = {0: None, 1: None}
        exp_free = [None, None, None, None]
        z_free = None
        mul_done = [None, None]
        tbcs_done = [None, None]
        pend_z = None
        epar = 0
        epoch = 0
        LA = 3

        def emit_zchain(st):
            nonlocal z_free
            e_, n_, qh_, atps_ = st["e"], st["n"], st["qh"], st["atps"]
            bcps_ = pB if atps_ is pA else pA
            tlnz = emit("scalar", lambda e: e.activation(recf[:, :], pB[0:1, 0:1024],
                                                         AF.Ln),
                        waits=[st["tlast"]], inc="s_s")
            trecb = emit("scalar", lambda e: e.activation(recb[:, :], recf[:, :], AF.Exp,
                                                          scale=-1.0),
                         waits=[tlnz], inc="s_s")
            z_free = tlnz
            for c in range(2):
                emit("tensor", lambda e, c=c, bcps_=bcps_: e.matmul(
                    bcps_[:, 1024 + c * 512:1024 + (c + 1) * 512], ONESR[:, :],
                    recb[0:1, c * 512:(c + 1) * 512], start=True, stop=True),
                    waits=[trecb, mul_done[(e_ + 1) % 2]], inc="s_t")
            tbc = p.tokt("s_t")
            tbcs = emit("vector", lambda e, bcps_=bcps_: e.tensor_copy(
                bcs[:, :], bcps_[:, 1024:2048]),
                waits=[tbc, mul_done[(e_ + 1) % 2]], inc="s_v")
            tbcs_done[e_ % 2] = tbcs
            tm = emit("vector", lambda e, n_=n_, qh_=qh_, atps_=atps_: e.tensor_mul(
                ATTNT[:, n_ * 2048 + qh_ * 1024: n_ * 2048 + (qh_ + 1) * 1024],
                atps_[:, 1024:2048], bcs[:, :]),
                waits=[tbcs], inc="s_v")
            mul_done[e_ % 2] = tm

        for n in range(NH):
            par = n % 2
            p.phase(BAR1 + [hload[n]])
            for qh in range(2):
                atps = pA if epoch % 2 == 0 else pB
                q0 = qh * 1024
                jmax = (qh + 1) * 8
                first_attn = [True, True]
                first_z = [True, True]
                plist = []
                for j in range(jmax):
                    qs_rel = max(0, j * 128 - q0)
                    pieces = []
                    if qs_rel < 512:
                        pieces.append((qs_rel, 512))
                    if qs_rel < 1024:
                        pieces.append((max(qs_rel, 512), 1024))
                    for (a, b) in pieces:
                        if b > a:
                            plist.append((j, a, b, j * 128 >= q0 and a == qs_rel))
                last_in_bank = {}
                for idx, (j, a, b, diag) in enumerate(plist):
                    last_in_bank[a // 512] = idx
                pinfo = {}
                for step in range(len(plist) + LA):
                    if step == 2 and pend_z is not None:
                        emit_zchain(pend_z)
                        pend_z = None
                    if step < len(plist):
                        j, a, b, diag = plist[step]
                        sslot = step % 2
                        sps, soff = pA, sslot * 512
                        ts = emit("tensor", lambda e, par=par, j=j, a=a, b=b, q0=q0,
                                  sps=sps, soff=soff: e.matmul(
                            sps[:, soff:soff + b - a], ktn[par][:, j * 128:(j + 1) * 128],
                            qtn[par][:, q0 + a:q0 + b], start=True, stop=True),
                            waits=[piece_free[sslot]], inc="s_t")
                        my_epar = epar
                        epar = (epar + 1) % 4
                        tex = emit("scalar", lambda e, a=a, b=b, my_epar=my_epar,
                                   sps=sps, soff=soff: e.activation(
                            expt[my_epar][:, 0:b - a], sps[:, soff:soff + b - a], AF.Exp,
                            bias=EXB[:, 0:1]),
                            waits=[ts, exp_free[my_epar]], inc="s_s")
                        piece_free[sslot] = tex
                        gate_tok = tex
                        if diag:
                            gate_tok = emit("vector", lambda e, my_epar=my_epar: e.tensor_mul(
                                expt[my_epar][:, 0:128], expt[my_epar][:, 0:128],
                                TRIUB[:, :]),
                                waits=[tex], inc="s_v")
                        pinfo[step] = (my_epar, gate_tok)
                    dstep = step - LA
                    if 0 <= dstep < len(plist):
                        j, a, b, diag = plist[dstep]
                        bank = a // 512
                        grp_stop = (last_in_bank[bank] == dstep)
                        my_epar, gate_tok = pinfo.pop(dstep)
                        emit("tensor", lambda e, a=a, b=b, my_epar=my_epar,
                             stz=first_z[bank], gsp=grp_stop: e.matmul(
                            pB[0:1, a:b], ONESC[:, :], expt[my_epar][:, 0:b - a],
                            start=stz, stop=gsp),
                            waits=[gate_tok] + ([z_free] if first_z[bank] else []),
                            inc="s_t")
                        first_z[bank] = False
                        tpv = emit("tensor", lambda e, par=par, j=j, a=a, b=b,
                                   my_epar=my_epar, stt=first_attn[bank],
                                   gsp=grp_stop, atps=atps: e.matmul(
                            atps[:, 1024 + a:1024 + b], vn[par][:, j * 128:(j + 1) * 128],
                            expt[my_epar][:, 0:b - a], start=stt, stop=gsp),
                            waits=([tbcs_done[(epoch + 1) % 2]]
                                   if first_attn[bank] else []),
                            inc="s_t")
                        first_attn[bank] = False
                        exp_free[my_epar] = tpv
                pend_z = {"e": epoch, "n": n, "qh": qh, "atps": atps,
                          "tlast": p.tokt("s_t")}
                epoch += 1
            if n + 2 < NH:
                hload[n + 2] = load_head(n + 2, [p.tokt("s_t")])
        emit_zchain(pend_z)
        pend_z = None

        p.phase([])
        BAR2 = barrier()
        if "ATT_t" in tap_outs:
            emit("sync", lambda e: e.dma_start(tap_outs["ATT_t"][:, :], ATTNT[:, :]),
                 waits=BAR2, inc="st_tap", dma=True)
        if nupto < 3:
            finish(p, emit)
            p.replay(block)
            return nc

        p.phase(BAR2)
        wotoks = [t_spw, None, None, None]
        wotoks[1] = emit("gpsimd", lambda e: e.dma_start(
            woq[1].ap().rearrange("p (a c) -> p a c", a=16),
            wor[:, :, 512:1024]), waits=BAR2, inc="w_wo1", dma=True)
        wotoks[2] = emit("gpsimd", lambda e: e.dma_start(
            woq[0].ap().rearrange("p (a c) -> p a c", a=16),
            wor[:, :, 1024:1536]), waits=BAR2, inc="w_wo2", dma=True)

        xt_free = [None, None, None]
        ev_prev = [None, None, None, None]
        sq_free = [None, None]
        gate_free = None
        gt8_free = None
        gsp_prev = None
        tt8_prev = [None, None]
        chain_done = [None, None]
        gsp_toks = []
        pending = None

        def gate_chain(i, par, tgl):
            nonlocal gt8_free, gsp_prev
            GLp = STATS[:, 48 + 8 * par: 56 + 8 * par]
            vchain(lambda e: e.reduce_max(STATS[:, 27:28], GLp, axis=AX.X),
                   extra=[tgl])
            vchain(lambda e, par=par: e.tensor_scalar(
                gwt[par][:, :], GLp, STATS[:, 27:28], None, ALU.is_equal),
                extra=[tt8_prev[par]])
            vchain(lambda e, par=par: e.scalar_tensor_tensor(
                STATS[:, 16:24], gwt[par][:, :], -1e30, GLp, ALU.mult, ALU.add))
            vchain(lambda e: e.reduce_max(STATS[:, 28:29], STATS[:, 16:24], axis=AX.X))
            t_d = vchain(lambda e: e.tensor_sub(STATS[:, 29:30], STATS[:, 28:29],
                                                STATS[:, 27:28]))
            te1 = emit("scalar", lambda e: e.activation(STATS[:, 40:41], STATS[:, 29:30],
                                                        AF.Exp, scale=-1.0),
                       waits=[t_d], inc="s_s")
            te2 = emit("scalar", lambda e: e.activation(STATS[:, 41:42], STATS[:, 29:30],
                                                        AF.Exp), waits=[te1], inc="s_s")
            vchain(lambda e: e.tensor_scalar_add(STATS[:, 42:44], STATS[:, 40:42], 1.0),
                   extra=[te2])
            vchain(lambda e: e.reciprocal(STATS[:, 44:46], STATS[:, 42:44]))
            vchain(lambda e, par=par: e.tensor_scalar(
                STATS[:, 32:40], GLp, STATS[:, 28:29], None, ALU.is_ge))
            vchain(lambda e, par=par: e.tensor_sub(STATS[:, 16:24], STATS[:, 32:40],
                                                   gwt[par][:, :]))
            vchain(lambda e: e.tensor_scalar(STATS[:, 16:24], STATS[:, 16:24],
                                             STATS[:, 44:45], None, ALU.mult))
            vchain(lambda e, par=par: e.tensor_scalar(
                gwt[par][:, :], gwt[par][:, :], STATS[:, 45:46], None, ALU.mult))
            t_w = vchain(lambda e, par=par: e.tensor_add(gwt[par][:, :], gwt[par][:, :],
                                                         STATS[:, 16:24]))
            chain_done[par] = t_w
            tt8 = emit("tensor", lambda e, par=par: e.matmul(
                pB[0:8, 512:640], gwt[par][:, :], IDF[:, :],
                start=True, stop=True, is_transpose=True),
                waits=[t_w, gt8_free], inc="s_t")
            tt8_prev[par] = tt8
            tv8 = emit("vector", lambda e: e.tensor_copy(gt8[:, :], pB[0:8, 512:640]),
                       waits=[tt8, gsp_prev], inc="s_v")
            gt8_free = tv8
            tsp = emit("sync", lambda e, i=i: e.dma_start(
                GSP[:, i * 128:(i + 1) * 128], gt8[:, :]),
                waits=[tv8], inc="st_gsp", dma=True)
            gsp_prev = tsp
            gsp_toks.append(tsp)

        def p3b_finish(pend):
            nonlocal ev_prev, gate_free, pending
            j, jpar, jbpar, jst = pend
            if pending is not None:
                gate_chain(*pending)
                pending = None
            tx = ln_rest(xt[jpar], sqb[jbpar], *jst, LNCOLS[j % 2])
            xt_free[jpar] = tx
            ev_prev, sq_free[jbpar] = emit_transposes(j, ev_prev, tx, sqb[jbpar])
            tev_all = ev_prev[3]
            for db in range(ND):
                emit("tensor", lambda e, db=db, j=j: e.matmul(
                    pB[:, 0:8],
                    XLNT[:, db * 2048 + j * 128: db * 2048 + (j + 1) * 128],
                    GWTB[:, db * 8:(db + 1) * 8],
                    start=(db == 0), stop=(db == ND - 1)),
                    waits=[tev_all, gate_free], inc="s_t")
            tg = p.tokt("s_t")
            tgl = emit("scalar", lambda e, jbpar=jbpar: e.copy(
                STATS[:, 48 + 8 * jbpar: 56 + 8 * jbpar], pB[:, 0:8]),
                waits=[tg, chain_done[jbpar]], inc="s_s")
            gate_free = tgl
            pending = (j, jbpar, tgl)

        psfree = {b: None for b in range(4)}
        xqfree = [None, None, None]
        x2free = [None, None, None]
        p3b_lts = {}
        p3b_st = {}

        def p3b_stats(j):
            p3b_st[j] = ln_stats(xt[j % 3], sqb[j % 2], p3b_lts[j],
                                 [sq_free[j % 2]], LNCOLS[j % 2])

        def p3b_fin(j):
            p3b_finish((j, j % 3, j % 2, p3b_st.pop(j)))

        for dq in range(4):
            wbuf = SPWOQ if dq == 0 else woq[dq % 2]
            p.phase(BAR2 + [wotoks[dq]])
            if dq == 3:
                tok_dq2 = p.tokt("s_t")
                ev_prev = [psfree[0], psfree[2], None, None]
            for i in range(NT):
                par = i % 3
                bank = 2 * (i % 2) + (dq % 2)
                pcol = bank * 512
                lt = emit("sync", lambda e, i=i, dq=dq, par=par: e.dma_start(
                    xq[par][:, :], xin[i * 128:(i + 1) * 128, dq * 512:(dq + 1) * 512]),
                    waits=[xqfree[par]], inc=f"l_xq{par}", dma=True)
                for nb in range(NH):
                    emit("tensor", lambda e, nb=nb, i=i, pcol=pcol, wbuf=wbuf: e.matmul(
                        pA[:, pcol:pcol + 512],
                        ATTNT[:, nb * 2048 + i * 128: nb * 2048 + (i + 1) * 128],
                        wbuf[:, nb * 512:(nb + 1) * 512],
                        start=(nb == 0), stop=(nb == NH - 1)),
                        waits=[psfree[bank]], inc="s_t")
                tmm = p.tokt("s_t")
                tadd = emit("vector", lambda e, par=par, pcol=pcol: e.tensor_add(
                    x2q[par][:, :], xq[par][:, :], pA[:, pcol:pcol + 512]),
                    waits=[tmm, lt, x2free[par]], inc="s_v")
                psfree[bank] = tadd
                xqfree[par] = tadd
                tst = emit("sync", lambda e, i=i, dq=dq, par=par: e.dma_start(
                    X2_sp[i * 128:(i + 1) * 128, dq * 512:(dq + 1) * 512], x2q[par][:, :]),
                    waits=[tadd], inc=f"st_x2{par}", dma=True)
                x2free[par] = tst
                if dq == 3:
                    if i >= 3:
                        p3b_fin(i - 3)
                    p3b_lts[i] = load_xt(
                        i, i % 3, X2_sp,
                        extra=[tst] + ([tok_dq2] if i < 2 else []))
                    if i >= 1:
                        p3b_stats(i - 1)
            if dq == 1:
                wotoks[3] = emit("gpsimd", lambda e: e.dma_start(
                    woq[1].ap().rearrange("p (a c) -> p a c", a=16),
                    wor[:, :, 1536:2048]),
                    waits=[p.tokt("s_t")], inc="w_wo1", dma=True)
        p3b_fin(NT - 3)
        p3b_stats(NT - 1)
        p3b_fin(NT - 2)
        p3b_fin(NT - 1)
        gate_chain(*pending)
        pending = None

        p.phase([])
        BAR3 = barrier(extra_st=True)
        w2r = w2.rearrange("E (a p) d -> E p a d", p=128)
        w2tok00 = emit("gpsimd", lambda e: e.dma_start(
            w2s[0].ap().rearrange("p (a c) -> p a c", a=NIB),
            w2r[0, :, :, 0:1024]), waits=BAR3, inc="w_w20", dma=True)
        if "X2_t" in tap_outs:
            emit("sync", lambda e: e.dma_start(tap_outs["X2_t"][:, :], X2_sp[:, :]),
                 waits=BAR3, inc="st_tap", dma=True)
        if nupto < 4:
            finish(p, emit)
            p.replay(block)
            return nc

        BAR3B = BAR3
        if "XLNT2_t" in tap_outs:
            emit("sync", lambda e: e.dma_start(tap_outs["XLNT2_t"][:, :], XLNT[:, :]),
                 waits=BAR3B, inc="st_tap", dma=True)
        if "GSP_t" in tap_outs:
            emit("sync", lambda e: e.dma_start(tap_outs["GSP_t"][:, :], GSP[:, :]),
                 waits=BAR3B, inc="st_tap", dma=True)
        if nupto < 5:
            finish(p, emit)
            p.replay(block)
            return nc

        p.phase(BAR3B)
        seq = [(pe, ib) for pe in range(NPE) for ib in range(NIB)]
        wtoks = {}
        wtoks[seq[0]] = w13_tok0
        wtoks[seq[1]] = w13_tok1

        h_free = None
        g_free = None
        silu_free = None
        afin_free = [None, None]
        grtok = [None, None]
        grb_free = [None, None]
        beb_use = [None, None]
        tbev_pe = {}

        def grb_load(pe):
            gpar = pe % 2
            grtok[gpar] = emit("gpsimd", lambda e, pe=pe, gpar=gpar: e.dma_start(
                grb[gpar][:, :], GSP[pe:pe + 1, :]),
                waits=[gsp_toks[-1], grb_free[gpar]], inc=f"w_gr{gpar}", dma=True)

        def emit_gate_bcast(pe, waits_extra=(), ps=None):
            gpar = pe % 2
            ps = pA if ps is None else ps
            for c in range(4):
                emit("tensor", lambda e, c=c, gpar=gpar, ps=ps: e.matmul(
                    ps[:, c * 512:(c + 1) * 512], ONESR[:, :],
                    grb[gpar][0:1, c * 512:(c + 1) * 512], start=True, stop=True),
                    waits=[grtok[gpar]] + list(waits_extra), inc="s_t")
            tbe = p.tokt("s_t")
            grb_free[gpar] = tbe
            tb = emit("scalar", lambda e, gpar=gpar, ps=ps: e.activation(
                beb[gpar][:, :], ps[:, :], AF.Copy),
                waits=[tbe, beb_use[gpar]], inc="s_s")
            tbev_pe[pe] = tb
            return tb

        p.phase(BAR3B)
        grb_load(0)
        emit_gate_bcast(0, ps=pB)
        g_free = tbev_pe[0]
        at0_done = None
        pre_asl = None
        pre_x2 = None
        for pe in range(NPE):
            p.phase(BAR3B)
            if pe == 9:
                pre_asl = emit("scalar", lambda e: e.dma_start(
                    aslg[0][:, :].rearrange("p (a c) -> p a c", a=NIB),
                    AT_sp[0, :, :, 0:512].transpose([1, 0, 2])),
                    waits=at0_done, inc="l_asl0", dma=True)
                pre_x2 = emit("sync", lambda e: e.dma_start(
                    x2b[0][:, :], X2_sp[0:128, 0:1024]),
                    waits=[beb_use[1]], inc="l_xq0", dma=True)
            for ib in range(NIB):
                gidx = pe * NIB + ib
                slot = gidx % 2
                for db in range(ND):
                    for c in range(4):
                        emit("tensor", lambda e, db=db, c=c, slot=slot: e.matmul(
                            pA[:, c * 512:(c + 1) * 512],
                            w1c[slot][:, db * 128:(db + 1) * 128],
                            XLNT[:, db * 2048 + c * 512: db * 2048 + (c + 1) * 512],
                            start=(db == 0), stop=(db == ND - 1)),
                            waits=[wtoks[(pe, ib)], h_free], inc="s_t")
                th = p.tokt("s_t")
                tsilu = emit("scalar", lambda e: e.activation(silu[:, :], pA[:, :], AF.Silu),
                             waits=[th, silu_free], inc="s_s")
                h_free = tsilu
                if ib == NIB - 2 and pe + 1 < 8:
                    grb_load(pe + 1)
                if ib == NIB - 1 and pe + 1 < 8:
                    emit_gate_bcast(pe + 1, waits_extra=[tsilu])
                    h_free = tbev_pe[pe + 1]
                for db in range(ND):
                    for c in range(4):
                        emit("tensor", lambda e, db=db, c=c, slot=slot: e.matmul(
                            pB[:, c * 512:(c + 1) * 512],
                            w3c[slot][:, db * 128:(db + 1) * 128],
                            XLNT[:, db * 2048 + c * 512: db * 2048 + (c + 1) * 512],
                            start=(db == 0), stop=(db == ND - 1)),
                            waits=[g_free], inc="s_t")
                tg2 = p.tokt("s_t")
                if gidx + 2 < len(seq):
                    nxt = seq[gidx + 2]
                    wtoks[nxt] = w13_load(nxt[0], nxt[1], (gidx + 2) % 2, [tg2])
                apar = gidx % 2
                if pe < 8:
                    gpar = pe % 2
                    ta = emit("vector", lambda e: e.tensor_mul(atmp[:, :], silu[:, :], pB[:, :]),
                              waits=[tg2, tsilu], inc="s_v")
                    tfin = emit("vector", lambda e, apar=apar, gpar=gpar: e.tensor_mul(
                        afin[apar][:, :], atmp[:, :], beb[gpar][:, :]),
                        waits=[ta, afin_free[apar], tbev_pe[pe]], inc="s_v")
                    beb_use[gpar] = tfin
                else:
                    tfin = emit("vector", lambda e, apar=apar: e.tensor_mul(
                        afin[apar][:, :], silu[:, :], pB[:, :]),
                        waits=[tg2, tsilu, afin_free[apar]], inc="s_v")
                g_free = tfin
                silu_free = tfin
                tsp = emit("sync", lambda e, pe=pe, ib=ib, apar=apar: e.dma_start(
                    AT_sp[pe, ib, :, :], afin[apar][:, :]),
                    waits=[tfin], inc=f"st_af{apar}", dma=True)
                afin_free[apar] = tsp
            if pe == 0:
                at0_done = [p.tokt("st_af0"), p.tokt("st_af1")]

        p.phase([])
        BAR5A = barrier(extra_st=True)
        if "AT_t" in tap_outs:
            emit("sync", lambda e: e.dma_start(tap_outs["AT_t"][:, :, :, :],
                                               AT_sp[:, :, :, :]),
                 waits=BAR5A, inc="st_tap", dma=True)
        if nupto < 6:
            finish(p, emit)
            p.replay(block)
            return nc

        sw2r = sw2.rearrange("(g a p) d -> g p a d", g=2, p=128)

        def w2_load(dh, pe, waits):
            if pe < 8:
                src = w2r[pe, :, :, dh * 1024:(dh + 1) * 1024]
            else:
                src = sw2r[pe - 8, :, :, dh * 1024:(dh + 1) * 1024]
            return emit("gpsimd", lambda e, src=src, pe=pe: e.dma_start(
                w2s[pe % 2].ap().rearrange("p (a c) -> p a c", a=NIB), src),
                waits=waits, inc=f"w_w2{pe % 2}", dma=True)

        yacc_done = [None] * NT
        w2s_free = [None, None]
        aslfree = [None, None]
        ps_free = {0: None, 1: None, 2: None, 3: None}
        x2b_free = [None, None]
        cur_lt = None
        cur_ap2 = 0
        for dh in range(2):
            p.phase(BAR5A)
            prev_out = p.tokt("st_out") if dh == 1 else None
            w2tok = {0: w2tok00 if dh == 0 else w2_load(dh, 0, [w2s_free[0]])}
            x2tok = [None, None]
            w2_consumed = {}
            for pe in range(NPE):
                p.phase(BAR5A + [w2tok[pe]])
                for k in range(NT):
                    kg, kk = k // 4, k % 4
                    apar = (pe * NT + k) % 4
                    psl = [(pA, 0), (pB, 0), (pA, 1024), (pB, 1024)][apar]
                    if pe == 0 and not (dh == 0 and k == 0):
                        x2tok[k % 2] = emit("sync", lambda e, k=k, dh=dh:
                                            e.dma_start(
                            x2b[k % 2][:, :],
                            X2_sp[k * 128:(k + 1) * 128, dh * 1024:(dh + 1) * 1024]),
                            waits=[x2b_free[k % 2]], inc=f"l_xq{k % 2}", dma=True)
                    elif pe == 0:
                        x2tok[0] = pre_x2
                    if kk == 0:
                        cur_ap2 = ((dh * NPE + pe) * 4 + kg) % 2
                        if dh == 0 and pe == 0 and kg == 0:
                            cur_lt = pre_asl
                        else:
                            cur_lt = emit("scalar", lambda e, pe=pe, kg=kg, dh=dh,
                                          ap2=cur_ap2: e.dma_start(
                                aslg[ap2][:, :].rearrange("p (a c) -> p a c", a=NIB),
                                AT_sp[pe, :, :, kg * 512:(kg + 1) * 512].transpose(
                                    [1, 0, 2])),
                                waits=[aslfree[cur_ap2]],
                                inc=f"l_asl{cur_ap2}", dma=True)
                    for ib in range(NIB):
                        for c in range(2):
                            emit("tensor", lambda e, ib=ib, c=c, kk=kk, pe=pe,
                                 ap2=cur_ap2, psl=psl: e.matmul(
                                psl[0][:, psl[1] + c * 512: psl[1] + (c + 1) * 512],
                                aslg[ap2][:, ib * 512 + kk * 128:
                                          ib * 512 + (kk + 1) * 128],
                                w2s[pe % 2][:, ib * 1024 + c * 512:
                                            ib * 1024 + (c + 1) * 512],
                                start=(ib == 0), stop=(ib == NIB - 1)),
                                waits=[cur_lt, ps_free[apar]], inc="s_t")
                    tmm = p.tokt("s_t")
                    if kk == 3:
                        aslfree[cur_ap2] = tmm
                    if pe == 0:
                        tac = emit("vector", lambda e, k=k, psl=psl, kpar=k % 2: e.tensor_add(
                            Yacc[k][:, :], x2b[kpar][:, :],
                            psl[0][:, psl[1]:psl[1] + 1024]),
                            waits=[tmm, x2tok[k % 2], prev_out, yacc_done[k]],
                            inc="s_v")
                        x2b_free[k % 2] = tac
                    else:
                        tac = emit("vector", lambda e, k=k, psl=psl: e.tensor_add(
                            Yacc[k][:, :], Yacc[k][:, :],
                            psl[0][:, psl[1]:psl[1] + 1024]),
                            waits=[tmm, yacc_done[k]], inc="s_v")
                    ps_free[apar] = tac
                    yacc_done[k] = tac
                    if pe == NPE - 1:
                        emit("sync", lambda e, k=k, dh=dh: e.dma_start(
                            out[k * 128:(k + 1) * 128, dh * 1024:(dh + 1) * 1024],
                            Yacc[k][:, :]),
                            waits=[tac], inc="st_out", dma=True)
                w2_consumed[pe] = p.tokt("s_t")
                if pe + 1 < NPE:
                    w2tok[pe + 1] = w2_load(
                        dh, pe + 1,
                        [w2_consumed[pe - 1]] if pe >= 1 else [w2s_free[1]])
            w2s_free = [w2_consumed[8], w2_consumed[NPE - 1]]

        finish(p, emit)
        p.replay(block)
    return nc


def finish(p, emit):
    final = [(s, p.cnt[s]) for s in SEM_NAMES if p.cnt[s] > 0]
    p.phase([])
    emit("sync", lambda e: None, waits=final)


def _make_consts():
    ident = np.eye(128, dtype=np.float32)
    triu = np.triu(np.ones((128, 128), dtype=np.float32))
    perm = np.zeros((128, 128), dtype=np.float32)
    for h in range(128):
        perm[h, h ^ 1] = 1.0
    i = np.arange(64, dtype=np.float64)
    ts = 10000.0 ** (i / 64.0)
    t = np.arange(S, dtype=np.float64)
    ang = t[None, :] / ts[:, None]
    cos = np.cos(ang)
    sin = np.sin(ang)
    cosf = np.repeat(cos, 2, axis=0).astype(np.float32)
    ssin = np.empty((128, S), dtype=np.float32)
    ssin[0::2, :] = -sin
    ssin[1::2, :] = sin
    return ident, triu, perm, cosf, ssin


def _in_maps(kw, x):
    ident, triu, perm, cosf, ssin = _make_consts()
    base = {
        "wq": np.ascontiguousarray(np.asarray(kw["L0_attn_wq"], dtype=np.float32).reshape(D, D)),
        "wk": np.ascontiguousarray(np.asarray(kw["L0_attn_wk"], dtype=np.float32).reshape(D, D)),
        "wv": np.ascontiguousarray(np.asarray(kw["L0_attn_wv"], dtype=np.float32).reshape(D, D)),
        "wo": np.ascontiguousarray(np.asarray(kw["L0_attn_wo"], dtype=np.float32).reshape(D, D)),
        "gwT": np.ascontiguousarray(np.asarray(kw["L0_ffn_gate_w"], dtype=np.float32).T),
        "w1": np.ascontiguousarray(np.asarray(kw["L0_ffn_w1"], dtype=np.float32)),
        "w3": np.ascontiguousarray(np.asarray(kw["L0_ffn_w3"], dtype=np.float32)),
        "w2": np.ascontiguousarray(np.asarray(kw["L0_ffn_w2"], dtype=np.float32)),
        "sw1": np.ascontiguousarray(np.asarray(kw["L0_ffn_sw1"], dtype=np.float32)),
        "sw3": np.ascontiguousarray(np.asarray(kw["L0_ffn_sw3"], dtype=np.float32)),
        "sw2": np.ascontiguousarray(np.asarray(kw["L0_ffn_sw2"], dtype=np.float32)),
        "cident": ident, "ctriu": triu, "cperm": perm,
        "ccos": cosf, "cssin": ssin,
    }
    return [dict(base, xin=np.ascontiguousarray(x[c])) for c in range(8)]


def kernel(L0_attn_x, L0_attn_wq, L0_attn_wk, L0_attn_wv, L0_attn_wo,
           L0_ffn_gate_w, L0_ffn_w1, L0_ffn_w3, L0_ffn_w2,
           L0_ffn_sw1, L0_ffn_sw3, L0_ffn_sw2):
    kw = dict(L0_attn_wq=L0_attn_wq, L0_attn_wk=L0_attn_wk, L0_attn_wv=L0_attn_wv,
              L0_attn_wo=L0_attn_wo, L0_ffn_gate_w=L0_ffn_gate_w, L0_ffn_w1=L0_ffn_w1,
              L0_ffn_w3=L0_ffn_w3, L0_ffn_w2=L0_ffn_w2, L0_ffn_sw1=L0_ffn_sw1,
              L0_ffn_sw3=L0_ffn_sw3, L0_ffn_sw2=L0_ffn_sw2)
    x = np.asarray(L0_attn_x, dtype=np.float32)
    nc = build()
    res = run_bass_kernel_spmd(nc, _in_maps(kw, x), core_ids=list(range(8)))
    return np.stack([res.results[c]["out"] for c in range(8)], axis=0)
